# version 04
# baseline (speedup 1.0000x reference)
import numpy as np
import ml_dtypes
import concourse.bass as bass
import concourse.mybir as mybir
from concourse.bass_utils import run_bass_kernel_spmd

F32 = mybir.dt.float32
BF16 = mybir.dt.bfloat16
I32 = mybir.dt.int32
AF = mybir.ActivationFunctionType
ALU = mybir.AluOpType
BFNP = ml_dtypes.bfloat16
AX = mybir.AxisListType.X

H, P, G, N, CS, KCONV, DIM = 128, 64, 8, 128, 128, 4, 4096
INTER = H * P
CONV_DIM = INTER + 2 * G * N
NPROJ = INTER + CONV_DIM + H
EPS = 1e-5
TOK = 2048
NCHUNK = TOK // CS
NKT = DIM // 128
NTILE = NPROJ // 128
N_HBC_T = CONV_DIM // 128
T_DT = N_HBC_T
T_GATE0 = N_HBC_T + 1


class Sem:
    def __init__(self, h):
        self.h = h
        self.v = 0


def inc(instr, s, n=1):
    instr.then_inc(s.h, n)
    s.v += n
    return instr


def build_program(taps=(), flags=()):
    nc = bass.Bass("TRN2", target_bir_lowering=False)

    d_xT = nc.dram_tensor("xT", [DIM, TOK + 3], F32, kind="ExternalInput")
    d_win = nc.dram_tensor("w_in", [NTILE, 128, NKT, 128], F32, kind="ExternalInput")
    d_wout = nc.dram_tensor("w_out", [32, 128, 64, 128], F32, kind="ExternalInput")
    d_wffn = nc.dram_tensor("w_ffn", [32, 128, NKT, 128], F32, kind="ExternalInput")
    d_cw = nc.dram_tensor("cw", [128, N_HBC_T * 4], F32, kind="ExternalInput")
    d_cb = nc.dram_tensor("cb", [128, N_HBC_T], F32, kind="ExternalInput")
    d_nw = nc.dram_tensor("nw", [128, 64], F32, kind="ExternalInput")
    d_fb = nc.dram_tensor("fb", [128, 32], F32, kind="ExternalInput")
    d_dtb = nc.dram_tensor("dtb", [128, 1], F32, kind="ExternalInput")
    d_alog = nc.dram_tensor("alog", [128, 1], F32, kind="ExternalInput")
    d_dbc = nc.dram_tensor("dbc", [128, 128], F32, kind="ExternalInput")
    d_msk = nc.dram_tensor("mskt", [128, 128], F32, kind="ExternalInput")
    d_cst = nc.dram_tensor("cst", [128, 8], F32, kind="ExternalInput")
    d_idf = nc.dram_tensor("idf", [128, 128], F32, kind="ExternalInput")
    d_onb = nc.dram_tensor("onb", [128, 128], BF16, kind="ExternalInput")
    d_idb = nc.dram_tensor("idb", [128, 128], BF16, kind="ExternalInput")
    d_mneg = nc.dram_tensor("mneg", [128, 512], BF16, kind="ExternalInput")
    d_onf = nc.dram_tensor("onf", [1, 128], F32, kind="ExternalInput")
    d_laT = nc.dram_tensor("laT", [3, DIM, 16], F32, kind="ExternalInput")
    d_lbT = nc.dram_tensor("lbT", [3, 16, DIM], F32, kind="ExternalInput")
    d_lsi = nc.dram_tensor("lsi", [1, 4], I32, kind="ExternalInput")
    d_out = nc.dram_tensor("out", [TOK, DIM], F32, kind="ExternalOutput")
    sp_hbc = nc.dram_tensor("sp_hbc", [CONV_DIM, TOK], BF16, kind="ExternalOutput" if "sp_hbc" in taps else "Internal")
    sp_gate = nc.dram_tensor("sp_gate", [INTER, TOK], BF16, kind="ExternalOutput" if "sp_gate" in taps else "Internal")
    sp_dt = nc.dram_tensor("sp_dt", [128, TOK], F32, kind="ExternalOutput" if "sp_dt" in taps else "Internal")
    sp_cumA = nc.dram_tensor("sp_cumA", [128, TOK], F32, kind="ExternalOutput" if "sp_cumA" in taps else "Internal")
    sp_z = nc.dram_tensor("sp_z", [TOK, INTER], BF16, kind="ExternalOutput" if "sp_z" in taps else "Internal")
    sp_r = nc.dram_tensor("sp_r", [DIM, 512], F32, kind="ExternalOutput" if "sp_r" in taps else "Internal")
    sp_x1 = nc.dram_tensor("sp_x1", [DIM, TOK], F32, kind="ExternalOutput" if "sp_x1" in taps else "Internal")
    d_dbg0 = nc.dram_tensor("dbg0", [128, 64], F32, kind="ExternalOutput" if "dbg0" in taps else "Internal")
    d_dbga = nc.dram_tensor("dbga", [128, 512], BF16, kind="ExternalOutput" if "dbg0" in taps else "Internal")
    cc_in = nc.dram_tensor("cc_in", [128, INTER], F32)
    cc_out = nc.dram_tensor("cc_out", [256, INTER], F32)

    from contextlib import ExitStack
    es = ExitStack()

    def SB(name, shape, dtype):
        return es.enter_context(nc.sbuf_tensor(name, shape, dtype))

    def sem(name):
        return Sem(es.enter_context(nc.semaphore(name)))

    g_fb = SB("g_fb", [128, 32], F32)
    g_dtb = SB("g_dtb", [128, 1], F32)
    g_alog = SB("g_alog", [128, 1], F32)
    g_acol = SB("g_acol", [128, 1], F32)
    g_dbc = SB("g_dbc", [128, 128], BF16)
    g_msk = SB("g_msk", [128, 128], F32)
    g_cst = SB("g_cst", [128, 8], F32)
    g_idf = SB("g_idf", [128, 128], F32)
    g_onb = SB("g_onb", [128, 128], BF16)
    g_idb = SB("g_idb", [128, 128], BF16)
    g_mneg = SB("g_mneg", [128, 512], BF16)
    g_onf = SB("g_onf", [1, 128], F32)
    g_aeff = SB("g_aeff", [128, 512], BF16)
    g_beff = SB("g_beff", [16, DIM], BF16)

    s_wld = sem("s_wld")
    s_hdma = sem("s_hdma")
    s_spill = sem("s_spill")
    s_out = sem("s_out")
    s_pe = sem("s_pe")
    s_dve = sem("s_dve")
    s_act = sem("s_act")
    s_pool = sem("s_pool")
    s_cc = sem("s_cc")
    s_bar = sem("s_bar")

    blk_cm = nc.Block()
    block = blk_cm.__enter__()
    nbar = [0]

    def barrier():
        nbar[0] += 1
        tgt = 5 * nbar[0]
        for e in ("tensor", "vector", "scalar", "gpsimd", "sync"):
            def f(eng, e=e):
                if e in ("vector", "scalar", "gpsimd"):
                    eng.drain().then_inc(s_bar.h, 1)
                else:
                    eng.sem_inc(s_bar.h, 1)
                eng.wait_ge(s_bar.h, tgt)
            getattr(block, e)(f)
        s_bar.v = tgt

    st0 = ExitStack()
    laT = st0.enter_context(nc.sbuf_tensor("p0_laT", [128, 3 * NKT * 16], F32))
    lbT = st0.enter_context(nc.sbuf_tensor("p0_lbT", [16, 3 * DIM], F32))
    p0m = st0.enter_context(nc.sbuf_tensor("p0_m", [128, 64], F32))
    lsi_i = st0.enter_context(nc.sbuf_tensor("p0_lsi", [1, 4], I32))
    tmpA = st0.enter_context(nc.sbuf_tensor("p0_tA", [128, 512], F32))
    tmpB = st0.enter_context(nc.sbuf_tensor("p0_tB", [16, DIM], F32))
    ps0 = st0.enter_context(nc.psum_tensor("p0_ps", [128, 512], F32))

    def f(g):
        for o, i in [
            (g_fb[:, :], d_fb.ap()),
            (g_dtb[:, :], d_dtb.ap()), (g_alog[:, :], d_alog.ap()),
            (g_dbc[:, :], d_dbc.ap()), (g_msk[:, :], d_msk.ap()),
            (g_cst[:, :], d_cst.ap()), (g_idf[:, :], d_idf.ap()),
            (g_onb[:, :], d_onb.ap()), (g_onf[:, :], d_onf.ap()),
            (g_idb[:, :], d_idb.ap()), (g_mneg[:, :], d_mneg.ap()),
            (lsi_i[:, :], d_lsi.ap()),
            (laT[:, :], bass.AP(d_laT, 0, [[16, 128], [DIM * 16, 3], [128 * 16, NKT], [1, 16]])),
            (lbT[:, :], bass.AP(d_lbT, 0, [[DIM, 16], [16 * DIM, 3], [1, DIM]])),
        ]:
            inc(g.dma_start(out=o, in_=i), s_wld, 16)
    block.gpsimd(f)
    n_p0 = s_wld.v

    def f(v):
        v.wait_ge(s_wld.h, n_p0)
        inc(v.tensor_copy(p0m[:1, 0:4], lsi_i[:, :]), s_dve, 1)
    block.vector(f)
    v_lsi = s_dve.v

    def f(t):
        t.wait_ge(s_dve.h, v_lsi)
        inc(t.matmul(ps0[:, 0:4], g_onf[:1, :], p0m[:1, 0:4], start=True, stop=True), s_pe, 1)
    block.tensor(f)
    pe_lsi = s_pe.v

    def f(v):
        v.wait_ge(s_pe.h, pe_lsi)

        def step(ins):
            inc(ins, s_dve, 1)
            v.wait_ge(s_dve.h, s_dve.v)
        step(v.tensor_scalar(p0m[:, 0:4], ps0[:, 0:4], g_cst[:, 0:1], None, ALU.is_le))
        step(v.reduce_sum(p0m[:, 4:5], p0m[:, 0:4], axis=AX))
        step(v.tensor_scalar(p0m[:, 5:6], p0m[:, 4:5], -1.0, None, ALU.add))
        step(v.tensor_scalar(p0m[:, 6:7], p0m[:, 5:6], 1.0, None, ALU.is_ge))
        step(v.tensor_scalar(p0m[:, 7:8], p0m[:, 5:6], -1.0, None, ALU.add))
        step(v.tensor_scalar(p0m[:, 7:8], p0m[:, 7:8], 0.0, None, ALU.max))
        step(v.tensor_scalar(p0m[:, 8:9], p0m[:, 7:8], 2.0, None, ALU.min))
        for k in range(3):
            step(v.tensor_scalar(p0m[:, 9 + k:10 + k], p0m[:, 8:9], float(k), None, ALU.is_equal))
            step(v.tensor_tensor(p0m[:, 32 + k:33 + k], p0m[:, 9 + k:10 + k],
                                 p0m[:, 6:7], ALU.mult))
        wc = p0m[:, 32:35]
        step(v.tensor_scalar(tmpA[:, :], laT[:, 0:512], wc[:, 0:1], None, ALU.mult))
        step(v.scalar_tensor_tensor(tmpA[:, :], laT[:, 512:1024], wc[:, 1:2], tmpA[:, :], ALU.mult, ALU.add))
        step(v.scalar_tensor_tensor(tmpA[:, :], laT[:, 1024:1536], wc[:, 2:3], tmpA[:, :], ALU.mult, ALU.add))
        step(v.tensor_copy(g_aeff[:, :], tmpA[:, :]))
        step(v.tensor_scalar(tmpB[:, :], lbT[:, 0:DIM], wc[:16, 0:1], None, ALU.mult))
        step(v.scalar_tensor_tensor(tmpB[:, :], lbT[:, DIM:2 * DIM], wc[:16, 1:2], tmpB[:, :], ALU.mult, ALU.add))
        step(v.scalar_tensor_tensor(tmpB[:, :], lbT[:, 2 * DIM:3 * DIM], wc[:16, 2:3], tmpB[:, :], ALU.mult, ALU.add))
        step(v.tensor_copy(g_beff[:, :], tmpB[:, :]))
    block.vector(f)

    def f(sc):
        sc.wait_ge(s_wld.h, n_p0)
        inc(sc.activation(g_acol[:, :], g_alog[:, :], AF.Exp), s_act, 1)
    block.scalar(f)
    a_exp = s_act.v

    def f(v):
        v.wait_ge(s_act.h, a_exp)
        inc(v.tensor_scalar(g_acol[:, :], g_acol[:, :], -1.0, None, ALU.mult), s_dve, 1)
    block.vector(f)

    if "dbg0" in taps:
        dv = s_dve.v

        def f(s, dv=dv):
            s.wait_ge(s_dve.h, dv)
            inc(s.dma_start(out=d_dbg0.ap(), in_=p0m[:, :]), s_spill, 16)
            inc(s.dma_start(out=d_dbga.ap(), in_=g_aeff[:, :]), s_spill, 16)
            s.wait_ge(s_spill.h, s_spill.v)
        block.sync(f)
    barrier()
    st0.close()
    st1 = ExitStack()
    xbT = st1.enter_context(nc.sbuf_tensor("p1_xbT", [128, NKT * (TOK + 3)], BF16))
    p1w0 = st1.enter_context(nc.sbuf_tensor("p1_w0", [128, NKT * 128], BF16))
    p1w1 = st1.enter_context(nc.sbuf_tensor("p1_w1", [128, NKT * 128], BF16))
    ci0 = st1.enter_context(nc.sbuf_tensor("p1_ci0", [128, TOK + 3], BF16))
    ci1 = st1.enter_context(nc.sbuf_tensor("p1_ci1", [128, TOK + 3], BF16))
    pa0 = st1.enter_context(nc.sbuf_tensor("p1_a0", [128, TOK], F32))
    pa1 = st1.enter_context(nc.sbuf_tensor("p1_a1", [128, TOK], F32))
    ph0 = st1.enter_context(nc.sbuf_tensor("p1_h0", [128, TOK], BF16))
    ph1 = st1.enter_context(nc.sbuf_tensor("p1_h1", [128, TOK], BF16))
    pg0 = st1.enter_context(nc.sbuf_tensor("p1_g0", [128, TOK], BF16))
    pg1 = st1.enter_context(nc.sbuf_tensor("p1_g1", [128, TOK], BF16))
    psA = st1.enter_context(nc.psum_tensor("p1_psA", [128, 512], F32))
    psB = st1.enter_context(nc.psum_tensor("p1_psB", [128, 512], F32))
    psH = st1.enter_context(nc.psum_tensor("p1_psH", [128, 512], F32))

    g_cw = st1.enter_context(nc.sbuf_tensor("g_cw", [128, N_HBC_T * 4], F32))
    g_cb = st1.enter_context(nc.sbuf_tensor("g_cb", [128, N_HBC_T], F32))
    g_nw = st1.enter_context(nc.sbuf_tensor("g_nw", [128, 64], F32))
    wbufs, cins, accs = (p1w0, p1w1), (ci0, ci1), (pa0, pa1)
    hbufs, gbufs, slots = (ph0, ph1), (pg0, pg1), (psA, psB)

    def f(g):
        for o, i in [(g_cw[:, :], d_cw.ap()), (g_cb[:, :], d_cb.ap()),
                     (g_nw[:, :], d_nw.ap())]:
            inc(g.dma_start(out=o, in_=i), s_wld, 16)
        for i in range(4):
            src = bass.AP(d_xT, i * 8 * 128 * (TOK + 3),
                          [[TOK + 3, 128], [128 * (TOK + 3), 8], [1, TOK + 3]])
            dst = bass.AP(xbT, i * 8 * (TOK + 3),
                          [[NKT * (TOK + 3), 128], [TOK + 3, 8], [1, TOK + 3]])
            inc(g.dma_start(out=dst, in_=src), s_wld, 16)
    block.gpsimd(f)
    x_ld = s_wld.v

    slot_free = [0, 0]
    halo_free = 0
    pe_tile_end = []
    conv_done = {}
    silu_done = {}
    spill_done = {}

    def tile_dst(ti):
        if ti < 64:
            return sp_hbc, ti * 128
        if ti < 72:
            return sp_hbc, INTER + (ti - 64) * 128
        if ti < 80:
            return sp_hbc, INTER + G * N + (ti - 72) * 128
        if ti == T_DT:
            return sp_dt, 0
        return sp_gate, (ti - T_GATE0) * 128

    for ti in range(NTILE):
        is_hbc = ti < N_HBC_T
        is_gate = ti >= T_GATE0
        pb = ti % 2

        def f(g, ti=ti, pb=pb):
            if ti >= 2:
                g.wait_ge(s_pe.h, pe_tile_end[ti - 2])
            src = bass.AP(d_win, ti * 128 * NKT * 128, [[NKT * 128, 128], [1, NKT * 128]])
            inc(g.dma_start(out=wbufs[pb][:, :], in_=src), s_wld, 16)
        block.gpsimd(f)
        w_rdy = s_wld.v

        for nb in range(4):
            sl = nb % 2

            def f(t, pb=pb, nb=nb, sl=sl, wr=w_rdy, sf=slot_free[sl]):
                t.wait_ge(s_wld.h, max(wr, x_ld))
                if sf:
                    t.wait_ge(s_act.h, sf)
                for kt in range(NKT):
                    ins = t.matmul(slots[sl][:, :],
                                   wbufs[pb][:, kt * 128:(kt + 1) * 128],
                                   xbT[:, kt * (TOK + 3) + 3 + nb * 512:
                                       kt * (TOK + 3) + 3 + (nb + 1) * 512],
                                   start=(kt == 0), stop=(kt == NKT - 1))
                    if kt == NKT - 1:
                        inc(ins, s_pe, 1)
            block.tensor(f)
            nb_pe = s_pe.v

            def f(sc, ti=ti, pb=pb, nb=nb, sl=sl, npe=nb_pe,
                  is_hbc=is_hbc, is_gate=is_gate):
                if is_hbc:
                    if nb == 0 and ti >= 2:
                        sc.wait_ge(s_dve.h, conv_done[ti - 2])
                elif is_gate:
                    if nb == 0 and ti >= 2 and (ti - 2) >= T_GATE0:
                        sc.wait_ge(s_spill.h, spill_done[ti - 2])
                else:
                    if nb == 0:
                        sc.wait_ge(s_act.h, silu_done[N_HBC_T - 1])
                sc.wait_ge(s_pe.h, npe)
                if is_hbc:
                    ins = sc.activation(cins[pb][:, 3 + nb * 512:3 + (nb + 1) * 512],
                                        slots[sl][:, :], AF.Copy)
                elif is_gate:
                    ins = sc.activation(gbufs[pb][:, nb * 512:(nb + 1) * 512],
                                        slots[sl][:, :], AF.Silu)
                else:
                    ins = sc.activation(accs[0][:, nb * 512:(nb + 1) * 512],
                                        slots[sl][:, :], AF.Copy)
                inc(ins, s_act, 1)
            block.scalar(f)
            slot_free[sl] = s_act.v

        if is_hbc:
            def f(t, pb=pb, hf=halo_free):
                if hf:
                    t.wait_ge(s_act.h, hf)
                for kt in range(NKT):
                    ins = t.matmul(psH[:, 0:3],
                                   wbufs[pb][:, kt * 128:(kt + 1) * 128],
                                   xbT[:, kt * (TOK + 3): kt * (TOK + 3) + 3],
                                   start=(kt == 0), stop=(kt == NKT - 1))
                    if kt == NKT - 1:
                        inc(ins, s_pe, 1)
            block.tensor(f)
            halo_pe = s_pe.v

            def f(sc, pb=pb, hp=halo_pe):
                sc.wait_ge(s_pe.h, hp)
                inc(sc.activation(cins[pb][:, 0:3], psH[:, 0:3], AF.Copy), s_act, 1)
            block.scalar(f)
            halo_free = s_act.v
            evac_done = s_act.v

            def f(v, ti=ti, pb=pb, ed=evac_done):
                v.wait_ge(s_act.h, ed)
                if ti >= 1 and (ti - 1) in silu_done:
                    v.wait_ge(s_act.h, silu_done[ti - 1])
                ci = cins[pb]
                a_, b_ = accs[pb], accs[1 - pb]
                v.tensor_scalar(a_[:, :], ci[:, 0:TOK], g_cw[:, ti * 4:ti * 4 + 1],
                                g_cb[:, ti:ti + 1], ALU.mult, ALU.add)
                v.scalar_tensor_tensor(b_[:, :], ci[:, 1:1 + TOK], g_cw[:, ti * 4 + 1:ti * 4 + 2],
                                       a_[:, :], ALU.mult, ALU.add)
                v.scalar_tensor_tensor(a_[:, :], ci[:, 2:2 + TOK], g_cw[:, ti * 4 + 2:ti * 4 + 3],
                                       b_[:, :], ALU.mult, ALU.add)
                ins = v.scalar_tensor_tensor(b_[:, :], ci[:, 3:3 + TOK], g_cw[:, ti * 4 + 3:ti * 4 + 4],
                                             a_[:, :], ALU.mult, ALU.add)
                inc(ins, s_dve, 1)
            block.vector(f)
            conv_done[ti] = s_dve.v

            def f(sc, ti=ti, pb=pb, cd=conv_done[ti]):
                sc.wait_ge(s_dve.h, cd)
                if ti >= 2 and (ti - 2) in spill_done:
                    sc.wait_ge(s_spill.h, spill_done[ti - 2])
                inc(sc.activation(hbufs[pb][:, :], accs[1 - pb][:, :], AF.Silu), s_act, 1)
            block.scalar(f)
            silu_done[ti] = s_act.v

            def f(s, ti=ti, pb=pb, sd=silu_done[ti]):
                s.wait_ge(s_act.h, sd)
                dst, r0 = tile_dst(ti)
                inc(s.dma_start(out=bass.AP(dst, r0 * TOK, [[TOK, 128], [1, TOK]]),
                                in_=hbufs[pb][:, :]), s_spill, 16)
            block.sync(f)
            spill_done[ti] = s_spill.v
        elif is_gate:
            gs_done = s_act.v

            def f(v, ti=ti, pb=pb, gd=gs_done):
                v.wait_ge(s_act.h, gd)
                gt = ti - T_GATE0
                ins = v.tensor_scalar(gbufs[pb][:, :], gbufs[pb][:, :],
                                      g_nw[:, gt:gt + 1], None, ALU.mult)
                inc(ins, s_dve, 1)
            block.vector(f)
            gn = s_dve.v

            def f(s, ti=ti, pb=pb, gn=gn):
                s.wait_ge(s_dve.h, gn)
                dst, r0 = tile_dst(ti)
                inc(s.dma_start(out=bass.AP(dst, r0 * TOK, [[TOK, 128], [1, TOK]]),
                                in_=gbufs[pb][:, :]), s_spill, 16)
            block.sync(f)
            spill_done[ti] = s_spill.v
        else:
            dt_done = s_act.v

            def f(s, de=dt_done):
                s.wait_ge(s_act.h, de)
                inc(s.dma_start(out=sp_dt.ap(), in_=accs[0][:, :]), s_spill, 16)
            block.sync(f)
            spill_done[ti] = s_spill.v

        pe_tile_end.append(s_pe.v)

    def f(s):
        s.wait_ge(s_spill.h, s_spill.v)
    block.sync(f)
    barrier()
    st1.close()
    st2 = ExitStack()
    S = st2.enter_context(nc.sbuf_tensor("p2_S", [128, INTER], F32))
    Sb = st2.enter_context(nc.sbuf_tensor("p2_Sb", [128, INTER], BF16))
    dts = st2.enter_context(nc.sbuf_tensor("p2_dt", [128, TOK], F32))
    cumA = st2.enter_context(nc.sbuf_tensor("p2_cA", [128, TOK], F32))
    Ftl = st2.enter_context(nc.sbuf_tensor("p2_F", [128, 128 * 8], F32))
    cl1p = st2.enter_context(nc.sbuf_tensor("p2_c1", [1, 128], F32))
    bcA = st2.enter_context(nc.psum_tensor("p2_bcA", [128, 512], F32))
    bcB = st2.enter_context(nc.psum_tensor("p2_bcB", [128, 512], F32))
    gmps = st2.enter_context(nc.psum_tensor("p2_gm", [128, 1024], F32))
    ydA = st2.enter_context(nc.psum_tensor("p2_ydA", [128, 512], F32))
    ydB = st2.enter_context(nc.psum_tensor("p2_ydB", [128, 512], F32))
    yoA = st2.enter_context(nc.psum_tensor("p2_yoA", [128, 512], F32))
    yoB = st2.enter_context(nc.psum_tensor("p2_yoB", [128, 512], F32))

    F3T_, F2T_, F5T_, E3_, F2_, F5_, DEC_, F3_ = range(8)
    FP = 128 * 8

    def Fv(i):
        return Ftl[:, i * 128:(i + 1) * 128]

    yd_ps, yo_ps, bc_ps = (ydA, ydB), (yoA, yoB), (bcA, bcB)

    war = {"e3_dve": 0, "dec_dve": 0, "f3t_pool": 0, "f5t_act": 0,
           "ftr_pe": 0, "bcA_dve": 0, "bcB_act": 0, "cl1p_pe": 0,
           "gmm_pool": 0}
    st_free = [0, 0]

    st2a = ExitStack()
    xa0 = st2a.enter_context(nc.sbuf_tensor("a_x0", [128, 9216], BF16))
    xa1 = st2a.enter_context(nc.sbuf_tensor("a_x1", [128, 9216], BF16))
    ab0 = st2a.enter_context(nc.sbuf_tensor("a_b0", [128, 512], BF16))
    ab1 = st2a.enter_context(nc.sbuf_tensor("a_b1", [128, 512], BF16))
    zz = st2a.enter_context(nc.sbuf_tensor("p2_zz", [128, 128], F32))
    adtc = st2a.enter_context(nc.sbuf_tensor("p2_ad", [128, 128], F32))
    xas = (xa0, xa1)
    bds = (ab0, ab1)

    def f(s):
        inc(s.dma_start(out=dts[:, :], in_=sp_dt.ap()), s_hdma, 16)
    block.sync(f)
    dt_ld = s_hdma.v

    def f(v, dl=dt_ld):
        v.wait_ge(s_hdma.h, dl)
        v.memset(zz[:, :], 0.0)
        inc(v.tensor_scalar(dts[:, :], dts[:, :], g_dtb[:, 0:1], None, ALU.add), s_dve, 1)
    block.vector(f)
    v_abs = s_dve.v

    def f(sc, va=v_abs):
        sc.wait_ge(s_dve.h, va)
        sc.activation(cumA[:, :], dts[:, :], AF.Abs)
        sc.activation(cumA[:, :], cumA[:, :], AF.Exp, scale=-1.0)
        inc(sc.activation(cumA[:, :], cumA[:, :], AF.Ln, bias=1.0), s_act, 1)
    block.scalar(f)
    a_sp = s_act.v

    def f(v, asp=a_sp):
        v.wait_ge(s_act.h, asp)
        v.tensor_scalar(dts[:, :], dts[:, :], 0.0, None, ALU.max)
        v.tensor_add(dts[:, :], dts[:, :], cumA[:, :])
        inc(v.memset(S[:, :], 0.0), s_dve, 1)
    block.vector(f)

    xa_free = [0, 0]
    bd_free = [0, 0]

    for c in range(NCHUNK):
        pb = c % 2
        csl = slice(c * 128, (c + 1) * 128)

        def f(s, c=c, pb=pb, xf=xa_free[pb]):
            if xf:
                s.wait_ge(s_pe.h, xf)
            inc(s.dma_start(out=xas[pb][:, :],
                            in_=bass.AP(sp_hbc, c * 128, [[TOK, 9216], [1, 128]]),
                            transpose=True), s_hdma, 16)
        block.sync(f)
        x_rdy = s_hdma.v

        def f(v, c=c, csl=csl):
            v.tensor_scalar(adtc[:, :], dts[:, csl], g_acol[:, 0:1], None, ALU.mult)
            ins = v.tensor_tensor_scan(cumA[:, csl], adtc[:, :], zz[:, :],
                                       0.0, ALU.add, ALU.add)
            inc(ins, s_dve, 1)
        block.vector(f)
        cum_rdy = s_dve.v

        def f(s, c=c, csl=csl, cr=cum_rdy, clf=war["cl1p_pe"]):
            s.wait_ge(s_dve.h, cr)
            inc(s.dma_start(out=bass.AP(sp_cumA, c * 128, [[TOK, 128], [1, 128]]),
                            in_=cumA[:, csl]), s_spill, 16)
            s.wait_ge(s_spill.h, s_spill.v)
            if clf:
                s.wait_ge(s_pe.h, clf)
            inc(s.dma_start(out=cl1p[:, :],
                            in_=bass.AP(sp_cumA, c * 128 + 127, [[0, 1], [TOK, 128], [1, 1]]),
                            allow_slow_non_contiguous=True), s_hdma, 16)
        block.sync(f)
        cl_rdy = s_hdma.v

        def f(sc, c=c, csl=csl, cr=cum_rdy, w1=war["e3_dve"]):
            sc.wait_ge(s_dve.h, cr)
            if w1:
                sc.wait_ge(s_dve.h, w1)
            ins = sc.activation(Fv(E3_), cumA[:, csl], AF.Exp, scale=-1.0,
                                bias=cumA[:, c * 128 + 127:c * 128 + 128])
            inc(ins, s_act, 1)
        block.scalar(f)
        e3_rdy = s_act.v

        def f(v, csl=csl, er=e3_rdy, w1=war["ftr_pe"]):
            v.wait_ge(s_act.h, er)
            if w1:
                v.wait_ge(s_pe.h, w1)
            inc(v.tensor_mul(Fv(F3_), Fv(E3_), dts[:, csl]), s_dve, 1)
        block.vector(f)
        f3_rdy = s_dve.v
        war["e3_dve"] = s_dve.v

        def f(t, fr=f3_rdy, clr=cl_rdy, wA=war["bcA_dve"], wB=war["bcB_act"]):
            if wA:
                t.wait_ge(s_dve.h, wA)
            t.wait_ge(s_dve.h, fr)
            inc(t.transpose(bcA[:, 0:128], Fv(F3_), g_idf[:, :]), s_pe, 1)
            if wB:
                t.wait_ge(s_act.h, wB)
            t.wait_ge(s_hdma.h, clr)
            inc(t.matmul(bcB[:, 0:128], g_onf[:1, :], cl1p[:1, :], start=True, stop=True),
                s_pe, 1)
        block.tensor(f)
        tr_rdy = s_pe.v - 1
        dec_mm = s_pe.v
        war["ftr_pe"] = tr_rdy
        war["cl1p_pe"] = s_pe.v

        def f(v, tr=tr_rdy, wp=war["f3t_pool"]):
            v.wait_ge(s_pe.h, tr)
            if wp:
                v.wait_ge(s_pool.h, wp)
            inc(v.tensor_copy(Fv(F3T_), bcA[:, 0:128]), s_dve, 1)
        block.vector(f)
        f3t_rdy = s_dve.v
        war["bcA_dve"] = s_dve.v

        def f(sc, dm=dec_mm, wd=war["dec_dve"]):
            sc.wait_ge(s_pe.h, dm)
            if wd:
                sc.wait_ge(s_dve.h, wd)
            inc(sc.activation(Fv(DEC_), bcB[:, 0:128], AF.Exp), s_act, 1)
        block.scalar(f)
        dec_rdy = s_act.v
        war["bcB_act"] = s_act.v

        for hb in range(32):
            bb = hb % 2
            g_ = hb // 4
            q = hb % 4

            def f(gp, hb=hb, g_=g_, bb=bb, pb=pb, fr=f3t_rdy, xr=x_rdy, bf=bd_free[bb]):
                gp.wait_ge(s_dve.h, fr)
                gp.wait_ge(s_hdma.h, xr)
                if bf:
                    gp.wait_ge(s_pe.h, bf)
                in0 = bass.AP(xas[pb], INTER + g_ * 128, [[9216, 128], [0, 4], [1, 128]])
                in1 = bass.AP(Ftl, F3T_ * 128 + hb * 4, [[FP, 128], [1, 4], [0, 128]])
                out = bass.AP(bds[bb], 0, [[512, 128], [128, 4], [1, 128]])
                inc(gp.tensor_tensor(out, in0, in1, ALU.mult), s_pool, 1)
            block.gpsimd(f)
            bd_rdy = s_pool.v
            war["f3t_pool"] = s_pool.v

            def f(t, hb=hb, bb=bb, pb=pb, q=q, br=bd_rdy, sf=st_free[q // 2]):
                t.wait_ge(s_pool.h, br)
                if sf:
                    t.wait_ge(s_dve.h, sf)
                for jj in range(4):
                    h = hb * 4 + jj
                    ins = t.matmul(gmps[:, q * 256 + jj * 64:q * 256 + (jj + 1) * 64],
                                   bds[bb][:, jj * 128:(jj + 1) * 128],
                                   xas[pb][:, h * 64:(h + 1) * 64], start=True, stop=True)
                    if jj == 3:
                        inc(ins, s_pe, 1)
            block.tensor(f)
            bd_free[bb] = s_pe.v
            if hb == 31:
                xa_free[pb] = s_pe.v

            if hb % 2 == 1:
                st_rdy = s_pe.v
                q0 = (hb - 1) % 4
                h0 = (hb - 1) * 4

                def f(v, sr=st_rdy, dr=dec_rdy, q0=q0, h0=h0):
                    v.wait_ge(s_pe.h, sr)
                    v.wait_ge(s_act.h, dr)
                    sv = bass.AP(S, h0 * 64, [[INTER, 128], [64, 8], [1, 64]])
                    din = bass.AP(Ftl, DEC_ * 128 + h0, [[FP, 128], [1, 8], [0, 64]])
                    v.tensor_tensor(sv, sv, din, ALU.mult)
                    ssl = S[:, h0 * 64:h0 * 64 + 512]
                    ins = v.tensor_tensor(ssl, ssl, gmps[:, q0 * 256:q0 * 256 + 512], ALU.add)
                    inc(ins, s_dve, 1)
                block.vector(f)
                st_free[q0 // 2] = s_dve.v
                war["dec_dve"] = s_dve.v

    s_fin = s_dve.v

    def f(s, sf=s_fin):
        s.wait_ge(s_dve.h, sf)
        inc(s.dma_start(out=cc_in.ap(), in_=S[:, :]), s_spill, 16)
    block.sync(f)
    ccin_rdy = s_spill.v

    def f(g, cr=ccin_rdy):
        g.wait_ge(s_spill.h, cr)
        ins = g.collective_compute(
            "AllGather", ALU.bypass,
            replica_groups=[[0, 1], [2, 3], [4, 5], [6, 7]],
            ins=[cc_in.ap().opt()], outs=[cc_out.ap().opt()])
        inc(ins, s_cc, 1)
    block.gpsimd(f)
    st2a.close()
    st2r = ExitStack()
    srecv = st2r.enter_context(nc.sbuf_tensor("p2_sr", [128, INTER], F32))
    def f(s):
        s.wait_ge(s_cc.h, 1)
        inc(s.dma_start(out=srecv[:, :],
                        in_=bass.AP(cc_out, 0, [[INTER, 128], [1, INTER]])), s_hdma, 16)
    block.sync(f)
    sr_rdy = s_hdma.v

    def f(v, sr=sr_rdy):
        v.wait_ge(s_hdma.h, sr)
        v.tensor_scalar(S[:, :], srecv[:, :], g_cst[:, 1:2], None, ALU.mult)
        inc(v.tensor_copy(Sb[:, :], S[:, :]), s_dve, 1)
    block.vector(f)
    sb_ready = s_dve.v
    barrier()
    st2r.close()
    st2b = ExitStack()
    xb0 = st2b.enter_context(nc.sbuf_tensor("b_x0", [128, 9216], BF16))
    xb1 = st2b.enter_context(nc.sbuf_tensor("b_x1", [128, 9216], BF16))
    bcm0 = st2b.enter_context(nc.sbuf_tensor("b_bc0", [128, 2048], BF16))
    bcm1 = st2b.enter_context(nc.sbuf_tensor("b_bc1", [128, 2048], BF16))
    gT = st2b.enter_context(nc.sbuf_tensor("b_gT", [128, INTER], BF16))
    zbb0 = st2b.enter_context(nc.sbuf_tensor("b_z0", [128, INTER], BF16))
    ysb = st2b.enter_context(nc.sbuf_tensor("b_y", [128, INTER], BF16))
    yosb = st2b.enter_context(nc.sbuf_tensor("b_yo", [128, 512], BF16))
    dxsb = st2b.enter_context(nc.sbuf_tensor("b_dx", [128, 1024], BF16))
    gmm = st2b.enter_context(nc.sbuf_tensor("b_gmm", [128, 1024], BF16))
    eE0 = st2b.enter_context(nc.sbuf_tensor("b_E0", [128, 512], BF16))
    eE1 = st2b.enter_context(nc.sbuf_tensor("b_E1", [128, 512], BF16))
    mM0 = st2b.enter_context(nc.sbuf_tensor("b_M0", [128, 512], BF16))
    mM1 = st2b.enter_context(nc.sbuf_tensor("b_M1", [128, 512], BF16))
    bb0 = st2b.enter_context(nc.sbuf_tensor("b_b0", [128, 512], BF16))
    bb1 = st2b.enter_context(nc.sbuf_tensor("b_b1", [128, 512], BF16))
    c1p0 = st2b.enter_context(nc.sbuf_tensor("b_c0", [1, 1024], F32))
    c1p1 = st2b.enter_context(nc.sbuf_tensor("b_c1", [1, 1024], F32))
    ssq = st2b.enter_context(nc.sbuf_tensor("b_ssq", [128, 4], F32))

    xbs = (xb0, xb1)
    bcms = (bcm0, bcm1)
    zbs = (zbb0, zbb0)
    Es = (eE0, eE1)
    Ms = (mM0, mM1)
    bds = (bb0, bb1)
    c1ps = (c1p0, c1p1)

    xb_free = [0, 0]
    bcm_free = [0, 0]
    gT_free = 0
    zb_free = [0, 0]
    E_free = [0, 0]
    M_free = [0, 0]
    bd_free = [0, 0]
    c1p_free = [0, 0]
    ydyo_free = [0, 0]
    bc_free = [0, 0]
    dx_free = [0, 0, 0, 0]
    sb_rdy_c = sb_ready

    for c in range(NCHUNK):
        pb = c % 2
        csl = slice(c * 128, (c + 1) * 128)

        def f(s, c=c, pb=pb, xf=xb_free[pb], bf=bcm_free[pb],
              gf=gT_free, clf=war["cl1p_pe"]):
            if xf:
                s.wait_ge(s_dve.h, xf)
            inc(s.dma_start(out=xbs[pb][:, :],
                            in_=bass.AP(sp_hbc, c * 128, [[TOK, 9216], [1, 128]]),
                            transpose=True), s_hdma, 16)
            if bf:
                s.wait_ge(s_pe.h, bf)
            inc(s.dma_start(out=bcms[pb][:, :],
                            in_=bass.AP(sp_hbc, INTER * TOK + c * 128,
                                        [[TOK, 128], [128 * TOK, 16], [1, 128]])),
                s_hdma, 16)
            if clf:
                s.wait_ge(s_pe.h, clf)
            inc(s.dma_start(out=cl1p[:, :],
                            in_=bass.AP(sp_cumA, c * 128 + 127, [[0, 1], [TOK, 128], [1, 1]]),
                            allow_slow_non_contiguous=True), s_hdma, 16)
            if gf:
                s.wait_ge(s_dve.h, gf)
            inc(s.dma_start(out=gT[:, :],
                            in_=bass.AP(sp_gate, c * 128, [[TOK, INTER], [1, 128]]),
                            transpose=True), s_hdma, 16)
        block.sync(f)
        gT_rdy = s_hdma.v
        cl_rdy = s_hdma.v - 16
        bcm_rdy = s_hdma.v - 32
        x_rdy = s_hdma.v - 48

        def f(sc, c=c, csl=csl, w1=war["ftr_pe"], w2=war["e3_dve"]):
            if w1:
                sc.wait_ge(s_pe.h, w1)
            if w2:
                sc.wait_ge(s_dve.h, w2)
            sc.activation(Fv(F2_), cumA[:, csl], AF.Exp)
            sc.activation(Fv(E3_), cumA[:, csl], AF.Exp, scale=-1.0,
                          bias=cumA[:, c * 128 + 127:c * 128 + 128])
            inc(sc.activation(Fv(F5_), dts[:, csl], AF.Ln), s_act, 1)
        block.scalar(f)
        fa_rdy = s_act.v

        def f(v, csl=csl, fa=fa_rdy):
            v.wait_ge(s_act.h, fa)
            v.tensor_mul(Fv(F3_), Fv(E3_), dts[:, csl])
            inc(v.tensor_sub(Fv(F5_), Fv(F5_), cumA[:, csl]), s_dve, 1)
        block.vector(f)
        ff_rdy = s_dve.v
        war["e3_dve"] = s_dve.v

        def f(t, fr=ff_rdy, clr=cl_rdy, b0=bc_free[0], b1=bc_free[1]):
            if b0:
                t.wait_ge(s_act.h, b0)
            t.wait_ge(s_dve.h, fr)
            t.transpose(bcA[:, 0:128], Fv(F3_), g_idf[:, :])
            t.transpose(bcA[:, 128:256], Fv(F2_), g_idf[:, :])
            inc(t.transpose(bcA[:, 256:384], Fv(F5_), g_idf[:, :]), s_pe, 1)
            if b1:
                t.wait_ge(s_act.h, b1)
            t.wait_ge(s_hdma.h, clr)
            inc(t.matmul(bcB[:, 0:128], g_onf[:1, :], cl1p[:1, :], start=True, stop=True),
                s_pe, 1)
        block.tensor(f)
        tr_rdy = s_pe.v - 1
        dec_mm = s_pe.v
        war["ftr_pe"] = tr_rdy
        war["cl1p_pe"] = s_pe.v

        def f(v, tr=tr_rdy, wp=war["f3t_pool"], wa=war["f5t_act"]):
            v.wait_ge(s_pe.h, tr)
            if wp:
                v.wait_ge(s_pool.h, wp)
            if wa:
                v.wait_ge(s_act.h, wa)
            v.tensor_copy(Fv(F3T_), bcA[:, 0:128])
            v.tensor_copy(Fv(F2T_), bcA[:, 128:256])
            inc(v.tensor_copy(Fv(F5T_), bcA[:, 256:384]), s_dve, 1)
        block.vector(f)
        ftr_cp = s_dve.v

        def f(sc, dm=dec_mm, wd=war["dec_dve"]):
            sc.wait_ge(s_pe.h, dm)
            if wd:
                sc.wait_ge(s_dve.h, wd)
            inc(sc.activation(Fv(DEC_), bcB[:, 0:128], AF.Exp), s_act, 1)
        block.scalar(f)
        dec_rdy = s_act.v
        war["bcB_act"] = s_act.v

        def f(t, br=bcm_rdy, pb=pb, s0=st_free[0], s1=st_free[1]):
            t.wait_ge(s_hdma.h, br)
            if s0:
                t.wait_ge(s_dve.h, s0)
            if s1:
                t.wait_ge(s_dve.h, s1)
            for g_ in range(8):
                ins = t.matmul(gmps[:, g_ * 128:(g_ + 1) * 128],
                               bcms[pb][:, g_ * 128:(g_ + 1) * 128],
                               bcms[pb][:, 1024 + g_ * 128:1024 + (g_ + 1) * 128],
                               start=True, stop=True)
                if g_ == 7:
                    inc(ins, s_pe, 1)
        block.tensor(f)
        gm_rdy = s_pe.v

        def f(v, gr=gm_rdy, wg=war["gmm_pool"]):
            v.wait_ge(s_pe.h, gr)
            if wg:
                v.wait_ge(s_pool.h, wg)
            for g_ in range(8):
                ins = v.tensor_tensor(gmm[:, g_ * 128:(g_ + 1) * 128],
                                      gmps[:, g_ * 128:(g_ + 1) * 128],
                                      g_msk[:, :], ALU.mult)
                if g_ == 7:
                    inc(ins, s_dve, 1)
        block.vector(f)
        gmm_rdy = s_dve.v

        seg_rdy = [0] * 16
        for q in (0, 1):
            def f(s, q=q, c=c, cf=c1p_free[q % 2]):
                if cf:
                    s.wait_ge(s_pe.h, cf)
                inc(s.dma_start(out=c1ps[q % 2][:, :],
                                in_=bass.AP(sp_cumA, (8 * q) * TOK + c * 128,
                                            [[0, 1], [TOK, 8], [1, 128]])),
                    s_hdma, 16)
            block.sync(f)
            seg_rdy[q] = s_hdma.v

        for hb in range(32):
            g_ = hb // 4
            eb = hb % 2
            j2 = (hb // 2) % 2
            q4 = hb % 4

            sg = hb // 2
            def f(t, hb=hb, eb=eb, sg=sg, sr=seg_rdy[sg], bf=bc_free[eb],
                  fc=ftr_cp, dr=dec_rdy):
                if hb == 0:
                    t.wait_ge(s_dve.h, fc)
                elif hb == 1:
                    t.wait_ge(s_act.h, dr)
                elif bf:
                    t.wait_ge(s_act.h, bf)
                t.wait_ge(s_hdma.h, sr)
                t.matmul(bc_ps[eb][:, :], g_onf[:1, :],
                         c1ps[sg % 2][:1, (hb % 2) * 512:(hb % 2 + 1) * 512],
                         start=True, stop=False)
                ins = t.matmul(bc_ps[eb][:, :], g_idb[:, :], g_mneg[:, :],
                               start=False, stop=True)
                inc(ins, s_pe, 1)
            block.tensor(f)
            bc_rdy = s_pe.v
            c1p_free[sg % 2] = s_pe.v

            def f(sc, hb=hb, eb=eb, br=bc_rdy, ef=E_free[eb], fc=ftr_cp):
                sc.wait_ge(s_pe.h, br)
                sc.wait_ge(s_dve.h, fc)
                if ef:
                    sc.wait_ge(s_pool.h, ef)
                for jj in range(4):
                    h = hb * 4 + jj
                    ins = sc.activation(Es[eb][:, jj * 128:(jj + 1) * 128],
                                        bc_ps[eb][:, jj * 128:(jj + 1) * 128], AF.Exp,
                                        bias=Ftl[:, F5T_ * 128 + h:F5T_ * 128 + h + 1])
                    if jj == 3:
                        inc(ins, s_act, 1)
            block.scalar(f)
            e_rdy = s_act.v
            bc_free[eb] = s_act.v
            war["f5t_act"] = s_act.v

            def f(gp, hb=hb, eb=eb, g_=g_, pb=pb, er=e_rdy, gr=gmm_rdy,
                  fc=ftr_cp, xr=x_rdy, mf=M_free[eb], bf=bd_free[eb],
                  df=dx_free[hb % 4]):
                gp.wait_ge(s_act.h, er)
                gp.wait_ge(s_dve.h, gr)
                if mf:
                    gp.wait_ge(s_pe.h, mf)
                in0 = bass.AP(Es[eb], 0, [[512, 128], [128, 4], [1, 128]])
                in1 = bass.AP(gmm, g_ * 128, [[1024, 128], [0, 4], [1, 128]])
                out = bass.AP(Ms[eb], 0, [[512, 128], [128, 4], [1, 128]])
                inc(gp.tensor_tensor(out, in0, in1, ALU.mult), s_pool, 1)
                gp.wait_ge(s_hdma.h, xr)
                if bf:
                    gp.wait_ge(s_pe.h, bf)
                in0 = bass.AP(xbs[pb], INTER + g_ * 128, [[9216, 128], [0, 4], [1, 128]])
                in1 = bass.AP(Ftl, F3T_ * 128 + hb * 4, [[FP, 128], [1, 4], [0, 128]])
                out = bass.AP(bds[eb], 0, [[512, 128], [128, 4], [1, 128]])
                inc(gp.tensor_tensor(out, in0, in1, ALU.mult), s_pool, 1)
                if df:
                    gp.wait_ge(s_dve.h, df)
                in0 = xbs[pb][:, hb * 256:(hb + 1) * 256]
                in1 = bass.AP(g_dbc, hb * 4, [[128, 128], [1, 4], [0, 64]])
                out = dxsb[:, (hb % 4) * 256:(hb % 4 + 1) * 256]
                inc(gp.tensor_tensor(out, in0, in1, ALU.mult), s_pool, 1)
            block.gpsimd(f)
            m_rdy = s_pool.v - 2
            bd_rdy = s_pool.v - 1
            dx_rdy = s_pool.v
            E_free[eb] = m_rdy
            war["f3t_pool"] = bd_rdy
            war["gmm_pool"] = m_rdy

            def f(t, hb=hb, eb=eb, j2=j2, g_=g_, q4=q4, pb=pb, br=bd_rdy,
                  yf=ydyo_free[j2], sf=st_free[q4 // 2], sbr=sb_rdy_c):
                t.wait_ge(s_pool.h, br)
                if hb == 0:
                    t.wait_ge(s_dve.h, sbr)
                if yf:
                    t.wait_ge(s_dve.h, yf)
                if sf:
                    t.wait_ge(s_dve.h, sf)
                for jj in range(4):
                    h = hb * 4 + jj
                    col8 = (h % 8) * 64
                    t.matmul(yd_ps[j2][:, col8:col8 + 64],
                             Ms[eb][:, jj * 128:(jj + 1) * 128],
                             xbs[pb][:, h * 64:(h + 1) * 64], start=True, stop=True)
                    t.matmul(yo_ps[j2][:, col8:col8 + 64],
                             bcms[pb][:, 1024 + g_ * 128:1024 + (g_ + 1) * 128],
                             Sb[:, h * 64:(h + 1) * 64], start=True, stop=True)
                    ins = t.matmul(gmps[:, q4 * 256 + jj * 64:q4 * 256 + (jj + 1) * 64],
                                   bds[eb][:, jj * 128:(jj + 1) * 128],
                                   xbs[pb][:, h * 64:(h + 1) * 64], start=True, stop=True)
                    if jj == 3:
                        inc(ins, s_pe, 1)
            block.tensor(f)
            y_rdy = s_pe.v
            M_free[eb] = s_pe.v
            bd_free[eb] = s_pe.v
            if hb == 31:
                bcm_free[pb] = s_pe.v

            if hb % 2 == 1:
                q0 = (hb - 1) % 4
                h0 = (hb - 1) * 4

                def f(v, j2=j2, q0=q0, h0=h0, yr=y_rdy, dxr=dx_rdy):
                    v.wait_ge(s_pe.h, yr)
                    v.wait_ge(s_pool.h, dxr)
                    yo = yosb[:, 0:512]
                    f2in = bass.AP(Ftl, F2T_ * 128 + h0, [[FP, 128], [1, 8], [0, 64]])
                    v.tensor_tensor(yo, yo_ps[j2][:, :], f2in, ALU.mult)
                    ysl = ysb[:, h0 * 64:h0 * 64 + 512]
                    v.scalar_tensor_tensor(ysl, yd_ps[j2][:, :], 1.0, yo, ALU.mult, ALU.add)
                    v.tensor_tensor(ysl, ysl, dxsb[:, q0 * 256:q0 * 256 + 512], ALU.add)
                    sv = bass.AP(S, h0 * 64, [[INTER, 128], [64, 8], [1, 64]])
                    din = bass.AP(Ftl, DEC_ * 128 + h0, [[FP, 128], [1, 8], [0, 64]])
                    v.tensor_tensor(sv, sv, din, ALU.mult)
                    ssl = S[:, h0 * 64:h0 * 64 + 512]
                    ins = v.tensor_tensor(ssl, ssl, gmps[:, q0 * 256:q0 * 256 + 512], ALU.add)
                    inc(ins, s_dve, 1)
                block.vector(f)
                ydyo_free[j2] = s_dve.v
                st_free[q0 // 2] = s_dve.v
                dx_free[q0] = s_dve.v
                dx_free[q0 + 1] = s_dve.v
                war["dec_dve"] = s_dve.v
                if hb == 31:
                    xb_free[pb] = s_dve.v

            if hb % 2 == 1 and (hb // 2) + 2 <= 15:
                def f(s, q=(hb // 2) + 2, c=c, cf=c1p_free[(hb // 2) % 2]):
                    if cf:
                        s.wait_ge(s_pe.h, cf)
                    inc(s.dma_start(out=c1ps[q % 2][:, :],
                                    in_=bass.AP(sp_cumA, (8 * q) * TOK + c * 128,
                                                [[0, 1], [TOK, 8], [1, 128]])),
                        s_hdma, 16)
                block.sync(f)
                seg_rdy[(hb // 2) + 2] = s_hdma.v

        y_done = s_dve.v

        def f(v, pb=pb, zf=zb_free[0]):
            if zf:
                v.wait_ge(s_spill.h, zf)
            ins = v.scalar_tensor_tensor(zbs[pb][:, :], ysb[:, :], 1.0, ysb[:, :],
                                         ALU.mult, ALU.mult, accum_out=ssq[:, 0:1])
            inc(ins, s_dve, 1)
        block.vector(f)
        ssq_rdy = s_dve.v

        def f(sc, sr=ssq_rdy):
            sc.wait_ge(s_dve.h, sr)
            sc.activation(ssq[:, 2:3], ssq[:, 0:1], AF.Ln, scale=g_cst[:, 4:5], bias=g_cst[:, 2:3])
            inc(sc.activation(ssq[:, 2:3], ssq[:, 2:3], AF.Exp, scale=g_cst[:, 3:4]), s_act, 1)
        block.scalar(f)
        rstd_rdy = s_act.v

        def f(v, pb=pb, rr=rstd_rdy, gr=gT_rdy):
            v.wait_ge(s_act.h, rr)
            v.wait_ge(s_hdma.h, gr)
            v.tensor_scalar(ysb[:, :], ysb[:, :], ssq[:, 2:3], None, ALU.mult)
            v.tensor_tensor(zbs[pb][:, :], ysb[:, :], gT[:, :], ALU.mult)
            inc(v.tensor_copy(Sb[:, :], S[:, :]), s_dve, 1)
        block.vector(f)
        z_rdy = s_dve.v
        gT_free = z_rdy
        sb_rdy_c = z_rdy

        def f(s, c=c, pb=pb, zr=z_rdy):
            s.wait_ge(s_dve.h, zr)
            inc(s.dma_start(out=bass.AP(sp_z, c * 128 * INTER, [[INTER, 128], [1, INTER]]),
                            in_=zbs[pb][:, :]), s_spill, 16)
        block.sync(f)
        zb_free[0] = s_spill.v

    def f(s):
        s.wait_ge(s_spill.h, s_spill.v)
    block.sync(f)
    barrier()
    st2b.close()
    st2.close()
    for B in range(4):
        st3o = ExitStack()
        x1a = st3o.enter_context(nc.sbuf_tensor(f"p3_x1a_{B}", [128, NKT * 512], BF16))
        st3 = ExitStack()
        zcm = st3.enter_context(nc.sbuf_tensor(f"p3_z_{B}", [128, 64 * 512], BF16))
        w0 = st3.enter_context(nc.sbuf_tensor(f"p3_w0_{B}", [128, 64 * 128], BF16))
        w1 = st3.enter_context(nc.sbuf_tensor(f"p3_w1_{B}", [128, 64 * 128], BF16))
        xr0 = st3.enter_context(nc.sbuf_tensor(f"p3_x0_{B}", [128, 512], F32))
        xr1 = st3.enter_context(nc.sbuf_tensor(f"p3_x1_{B}", [128, 512], F32))
        rb0 = st3.enter_context(nc.sbuf_tensor(f"p3_r0_{B}", [128, 512], F32))
        rb1 = st3.enter_context(nc.sbuf_tensor(f"p3_r1_{B}", [128, 512], F32))
        rbb0 = st3.enter_context(nc.sbuf_tensor(f"p3_rb0_{B}", [128, 512], BF16))
        rbb1 = st3.enter_context(nc.sbuf_tensor(f"p3_rb1_{B}", [128, 512], BF16))
        r2b0 = st3.enter_context(nc.sbuf_tensor(f"p3_q0_{B}", [128, 512], BF16))
        r2b1 = st3.enter_context(nc.sbuf_tensor(f"p3_q1_{B}", [128, 512], BF16))
        stt = st3.enter_context(nc.sbuf_tensor(f"p3_st_{B}", [128, 2048], F32))
        rl0 = st3.enter_context(nc.sbuf_tensor(f"p3_l0_{B}", [128, 512], F32))
        rl1 = st3.enter_context(nc.sbuf_tensor(f"p3_l1_{B}", [128, 512], F32))
        po0 = st3.enter_context(nc.psum_tensor(f"p3_po0_{B}", [128, 512], F32))
        po1 = st3.enter_context(nc.psum_tensor(f"p3_po1_{B}", [128, 512], F32))
        st1 = st3.enter_context(nc.psum_tensor(f"p3_s1_{B}", [128, 512], F32))
        st2 = st3.enter_context(nc.psum_tensor(f"p3_s2_{B}", [128, 512], F32))

        wb, xrs, rbs = (w0, w1), (xr0, xr1), (rb0, rb1)
        rbbs, r2bs, rls, pos = (rbb0, rbb1), (r2b0, r2b1), (rl0, rl1), (po0, po1)

        def f(s, B=B):
            for kt in range(64):
                inc(s.dma_start(out=zcm[:, kt * 512:(kt + 1) * 512],
                                in_=bass.AP(sp_z, (B * 512) * INTER + kt * 128,
                                            [[INTER, 512], [1, 128]]),
                                transpose=True), s_hdma, 16)
        block.sync(f)
        z_rdy = s_hdma.v

        po_free = [0, 0]
        po_done = {}
        rb_done = {}
        rbb_done = {}
        rbb_cons = {}
        rspill = {}

        for dm in range(32):
            pb = dm % 2

            def f(g, dm=dm, pb=pb):
                if dm >= 2:
                    g.wait_ge(s_pe.h, po_done[dm - 2])
                src = bass.AP(d_wout, dm * 128 * 64 * 128, [[64 * 128, 128], [1, 64 * 128]])
                inc(g.dma_start(out=wb[pb][:, :], in_=src), s_wld, 16)
            block.gpsimd(f)
            w_rdy = s_wld.v

            def f(s, dm=dm, pb=pb, B=B):
                if dm >= 2:
                    s.wait_ge(s_dve.h, rb_done[dm - 2])
                inc(s.dma_start(out=xrs[pb][:, :],
                                in_=bass.AP(d_xT, dm * 128 * (TOK + 3) + 3 + B * 512,
                                            [[TOK + 3, 128], [1, 512]])), s_hdma, 16)
            block.sync(f)
            xr_rdy = s_hdma.v

            def f(t, dm=dm, pb=pb, wr=w_rdy, zr=z_rdy, pf=po_free[pb]):
                t.wait_ge(s_wld.h, wr)
                if dm == 0:
                    t.wait_ge(s_hdma.h, zr)
                if pf:
                    t.wait_ge(s_dve.h, pf)
                for kt in range(64):
                    ins = t.matmul(pos[pb][:, :], wb[pb][:, kt * 128:(kt + 1) * 128],
                                   zcm[:, kt * 512:(kt + 1) * 512],
                                   start=(kt == 0), stop=(kt == 63))
                    if kt == 63:
                        inc(ins, s_pe, 1)
            block.tensor(f)
            po_done[dm] = s_pe.v

            if dm >= 1:
                def f(t, dm=dm, pb=pb, rc=rbb_done[dm - 1]):
                    t.wait_ge(s_act.h, rc)
                    t.matmul(st1[:, :], g_onb[:, :], rbbs[1 - pb][:, :],
                             start=(dm - 1 == 0), stop=False, skip_group_check=True)
                    ins = t.matmul(st2[:, :], g_onb[:, :], r2bs[1 - pb][:, :],
                                   start=(dm - 1 == 0), stop=False, skip_group_check=True)
                    inc(ins, s_pe, 1)
                block.tensor(f)
                rbb_cons[dm - 1] = s_pe.v

            def f(v, dm=dm, pb=pb, pr=po_done[dm], xr=xr_rdy):
                v.wait_ge(s_pe.h, pr)
                v.wait_ge(s_hdma.h, xr)
                if dm >= 2:
                    v.wait_ge(s_spill.h, rspill[dm - 2])
                inc(v.tensor_add(rbs[pb][:, :], pos[pb][:, :], xrs[pb][:, :]), s_dve, 1)
            block.vector(f)
            rb_done[dm] = s_dve.v
            po_free[pb] = s_dve.v

            def f(sc, dm=dm, pb=pb, rd=rb_done[dm]):
                sc.wait_ge(s_dve.h, rd)
                if dm >= 2:
                    sc.wait_ge(s_pe.h, rbb_cons[dm - 2])
                sc.activation(rbbs[pb][:, :], rbs[pb][:, :], AF.Copy)
                inc(sc.activation(r2bs[pb][:, :], rbs[pb][:, :], AF.Square), s_act, 1)
            block.scalar(f)
            rbb_done[dm] = s_act.v

            def f(s, dm=dm, pb=pb, rd=rb_done[dm]):
                s.wait_ge(s_dve.h, rd)
                inc(s.dma_start(out=bass.AP(sp_r, dm * 128 * 512, [[512, 128], [1, 512]]),
                                in_=rbs[pb][:, :]), s_spill, 16)
            block.sync(f)
            rspill[dm] = s_spill.v

        def f(t, rc=rbb_done[31]):
            t.wait_ge(s_act.h, rc)
            t.matmul(st1[:, :], g_onb[:, :], rbbs[1][:, :],
                     start=False, stop=True, skip_group_check=True)
            inc(t.matmul(st2[:, :], g_onb[:, :], r2bs[1][:, :],
                         start=False, stop=True, skip_group_check=True), s_pe, 1)
        block.tensor(f)
        st_done = s_pe.v
        rbb_cons[31] = s_pe.v

        def f(v, sd=st_done):
            v.wait_ge(s_pe.h, sd)
            v.tensor_scalar(stt[:, 0:512], st1[:, :], 1.0 / DIM, None, ALU.mult)
            inc(v.tensor_scalar(stt[:, 512:1024], st2[:, :], 1.0 / DIM, None, ALU.mult), s_dve, 1)
            v.wait_ge(s_dve.h, s_dve.v)
            inc(v.tensor_mul(stt[:, 1024:1536], stt[:, 0:512], stt[:, 0:512]), s_dve, 1)
            v.wait_ge(s_dve.h, s_dve.v)
            inc(v.tensor_sub(stt[:, 1024:1536], stt[:, 512:1024], stt[:, 1024:1536]),
                s_dve, 1)
        block.vector(f)
        var_rdy = s_dve.v

        def f(sc, vr=var_rdy):
            sc.wait_ge(s_dve.h, vr)
            inc(sc.activation(stt[:, 1536:2048], stt[:, 1024:1536], AF.Ln, bias=g_cst[:, 2:3]), s_act, 1)
            sc.wait_ge(s_act.h, s_act.v)
            inc(sc.activation(stt[:, 1536:2048], stt[:, 1536:2048], AF.Exp, scale=g_cst[:, 3:4]),
                s_act, 1)
        block.scalar(f)
        rstd_rdy = s_act.v

        x1_done = {}
        x1spill = {}
        for dm in range(32):
            pb = dm % 2

            def f(s, dm=dm, pb=pb, sp=rspill[dm]):
                s.wait_ge(s_spill.h, sp)
                if dm >= 2:
                    s.wait_ge(s_dve.h, x1_done[dm - 2])
                    s.wait_ge(s_spill.h, x1spill[dm - 2])
                inc(s.dma_start(out=rls[pb][:, :],
                                in_=bass.AP(sp_r, dm * 128 * 512, [[512, 128], [1, 512]])),
                    s_hdma, 16)
            block.sync(f)
            rl_rdy = s_hdma.v

            def f(v, dm=dm, pb=pb, rr=rl_rdy, sr=rstd_rdy):
                v.wait_ge(s_hdma.h, rr)
                v.wait_ge(s_act.h, sr)
                v.tensor_sub(rls[pb][:, :], rls[pb][:, :], stt[:, 0:512])
                v.tensor_mul(rls[pb][:, :], rls[pb][:, :], stt[:, 1536:2048])
                inc(v.tensor_copy(x1a[:, dm * 512:(dm + 1) * 512], rls[pb][:, :]), s_dve, 1)
            block.vector(f)
            x1_done[dm] = s_dve.v

            def f(s, dm=dm, pb=pb, xd=x1_done[dm], B=B):
                s.wait_ge(s_dve.h, xd)
                inc(s.dma_start(out=bass.AP(sp_x1, dm * 128 * TOK + B * 512,
                                            [[TOK, 128], [1, 512]]),
                                in_=rls[pb][:, :]), s_spill, 16)
            block.sync(f)
            x1spill[dm] = s_spill.v

        def f(s):
            s.wait_ge(s_spill.h, s_spill.v)
        block.sync(f)
        barrier()
        st3.close()
        st4 = ExitStack()
        w0 = st4.enter_context(nc.sbuf_tensor(f"p4_w0_{B}", [128, NKT * 128], BF16))
        w1 = st4.enter_context(nc.sbuf_tensor(f"p4_w1_{B}", [128, NKT * 128], BF16))
        lmb = st4.enter_context(nc.sbuf_tensor(f"p4_lm_{B}", [16, 512], BF16))
        xr0 = st4.enter_context(nc.sbuf_tensor(f"p4_x0_{B}", [128, 512], F32))
        xr1 = st4.enter_context(nc.sbuf_tensor(f"p4_x1_{B}", [128, 512], F32))
        rb0 = st4.enter_context(nc.sbuf_tensor(f"p4_r0_{B}", [128, 512], F32))
        rb1 = st4.enter_context(nc.sbuf_tensor(f"p4_r1_{B}", [128, 512], F32))
        rbb0 = st4.enter_context(nc.sbuf_tensor(f"p4_rb0_{B}", [128, 512], BF16))
        rbb1 = st4.enter_context(nc.sbuf_tensor(f"p4_rb1_{B}", [128, 512], BF16))
        r2b0 = st4.enter_context(nc.sbuf_tensor(f"p4_q0_{B}", [128, 512], BF16))
        r2b1 = st4.enter_context(nc.sbuf_tensor(f"p4_q1_{B}", [128, 512], BF16))
        stt = st4.enter_context(nc.sbuf_tensor(f"p4_st_{B}", [128, 2048], F32))
        rl0 = st4.enter_context(nc.sbuf_tensor(f"p4_l0_{B}", [128, 512], F32))
        rl1 = st4.enter_context(nc.sbuf_tensor(f"p4_l1_{B}", [128, 512], F32))
        ev0 = st4.enter_context(nc.sbuf_tensor(f"p4_e0_{B}", [128, 512], F32))
        ev1 = st4.enter_context(nc.sbuf_tensor(f"p4_e1_{B}", [128, 512], F32))
        po0 = st4.enter_context(nc.psum_tensor(f"p4_po0_{B}", [128, 512], F32))
        po1 = st4.enter_context(nc.psum_tensor(f"p4_po1_{B}", [128, 512], F32))
        st1 = st4.enter_context(nc.psum_tensor(f"p4_s1_{B}", [128, 512], F32))
        st2 = st4.enter_context(nc.psum_tensor(f"p4_s2_{B}", [128, 512], F32))
        lmp = st4.enter_context(nc.psum_tensor(f"p4_lp_{B}", [16, 512], F32))
        tp0 = st4.enter_context(nc.psum_tensor(f"p4_t0_{B}", [128, 128], F32))
        tp1 = st4.enter_context(nc.psum_tensor(f"p4_t1_{B}", [128, 128], F32))

        wb, xrs, rbs = (w0, w1), (xr0, xr1), (rb0, rb1)
        rbbs, r2bs, rls = (rbb0, rbb1), (r2b0, r2b1), (rl0, rl1)
        pos, evs, tps = (po0, po1), (ev0, ev1), (tp0, tp1)

        def f(t):
            for kt in range(NKT):
                ins = t.matmul(lmp[:, :], g_aeff[:, kt * 16:(kt + 1) * 16],
                               x1a[:, kt * 512:(kt + 1) * 512],
                               start=(kt == 0), stop=(kt == NKT - 1))
                if kt == NKT - 1:
                    inc(ins, s_pe, 1)
        block.tensor(f)
        lm_rdy = s_pe.v

        def f(v, lr=lm_rdy):
            v.wait_ge(s_pe.h, lr)
            inc(v.tensor_copy(lmb[:, :], lmp[:, :]), s_dve, 1)
        block.vector(f)
        lmb_rdy = s_dve.v

        po_free = [0, 0]
        po_done = {}
        rb_done = {}
        rbb_done = {}
        rbb_cons = {}
        rspill = {}
        for dm in range(32):
            pb = dm % 2

            def f(g, dm=dm, pb=pb):
                if dm >= 2:
                    g.wait_ge(s_pe.h, po_done[dm - 2])
                src = bass.AP(d_wffn, dm * 128 * NKT * 128, [[NKT * 128, 128], [1, NKT * 128]])
                inc(g.dma_start(out=wb[pb][:, :], in_=src), s_wld, 16)
            block.gpsimd(f)
            w_rdy = s_wld.v

            def f(s, dm=dm, pb=pb, B=B):
                if dm >= 2:
                    s.wait_ge(s_dve.h, rb_done[dm - 2])
                inc(s.dma_start(out=xrs[pb][:, :],
                                in_=bass.AP(sp_x1, dm * 128 * TOK + B * 512,
                                            [[TOK, 128], [1, 512]])), s_hdma, 16)
            block.sync(f)
            xr_rdy = s_hdma.v

            def f(t, dm=dm, pb=pb, wr=w_rdy, lr=lmb_rdy, pf=po_free[pb]):
                t.wait_ge(s_wld.h, wr)
                t.wait_ge(s_dve.h, lr)
                if pf:
                    t.wait_ge(s_dve.h, pf)
                nl = "no_lora" in flags
                for kt in range(NKT):
                    ins = t.matmul(pos[pb][:, :], wb[pb][:, kt * 128:(kt + 1) * 128],
                                   x1a[:, kt * 512:(kt + 1) * 512],
                                   start=(kt == 0), stop=(nl and kt == NKT - 1))
                    if nl and kt == NKT - 1:
                        inc(ins, s_pe, 1)
                if not nl:
                    ins = t.matmul(pos[pb][:, :], g_beff[:, dm * 128:(dm + 1) * 128],
                                   lmb[:, :], start=False, stop=True)
                    inc(ins, s_pe, 1)
            block.tensor(f)
            po_done[dm] = s_pe.v

            if dm >= 1:
                def f(t, dm=dm, pb=pb, rc=rbb_done[dm - 1]):
                    t.wait_ge(s_act.h, rc)
                    t.matmul(st1[:, :], g_onb[:, :], rbbs[1 - pb][:, :],
                             start=(dm - 1 == 0), stop=False, skip_group_check=True)
                    ins = t.matmul(st2[:, :], g_onb[:, :], r2bs[1 - pb][:, :],
                                   start=(dm - 1 == 0), stop=False, skip_group_check=True)
                    inc(ins, s_pe, 1)
                block.tensor(f)
                rbb_cons[dm - 1] = s_pe.v

            def f(v, dm=dm, pb=pb, pr=po_done[dm], xr=xr_rdy):
                v.wait_ge(s_pe.h, pr)
                v.wait_ge(s_hdma.h, xr)
                if dm >= 2:
                    v.wait_ge(s_spill.h, rspill[dm - 2])
                ins = v.scalar_tensor_tensor(rbs[pb][:, :], pos[pb][:, :],
                                             g_fb[:, dm:dm + 1], xrs[pb][:, :],
                                             ALU.add, ALU.add)
                inc(ins, s_dve, 1)
            block.vector(f)
            rb_done[dm] = s_dve.v
            po_free[pb] = s_dve.v

            def f(sc, dm=dm, pb=pb, rd=rb_done[dm]):
                sc.wait_ge(s_dve.h, rd)
                if dm >= 2:
                    sc.wait_ge(s_pe.h, rbb_cons[dm - 2])
                sc.activation(rbbs[pb][:, :], rbs[pb][:, :], AF.Copy)
                inc(sc.activation(r2bs[pb][:, :], rbs[pb][:, :], AF.Square), s_act, 1)
            block.scalar(f)
            rbb_done[dm] = s_act.v

            def f(s, dm=dm, pb=pb, rd=rb_done[dm]):
                s.wait_ge(s_dve.h, rd)
                inc(s.dma_start(out=bass.AP(sp_r, dm * 128 * 512, [[512, 128], [1, 512]]),
                                in_=rbs[pb][:, :]), s_spill, 16)
            block.sync(f)
            rspill[dm] = s_spill.v

        def f(t, rc=rbb_done[31]):
            t.wait_ge(s_act.h, rc)
            t.matmul(st1[:, :], g_onb[:, :], rbbs[1][:, :],
                     start=False, stop=True, skip_group_check=True)
            inc(t.matmul(st2[:, :], g_onb[:, :], r2bs[1][:, :],
                         start=False, stop=True, skip_group_check=True), s_pe, 1)
        block.tensor(f)
        st_done = s_pe.v
        rbb_cons[31] = s_pe.v

        def f(v, sd=st_done):
            v.wait_ge(s_pe.h, sd)
            v.tensor_scalar(stt[:, 0:512], st1[:, :], 1.0 / DIM, None, ALU.mult)
            inc(v.tensor_scalar(stt[:, 512:1024], st2[:, :], 1.0 / DIM, None, ALU.mult), s_dve, 1)
            v.wait_ge(s_dve.h, s_dve.v)
            inc(v.tensor_mul(stt[:, 1024:1536], stt[:, 0:512], stt[:, 0:512]), s_dve, 1)
            v.wait_ge(s_dve.h, s_dve.v)
            inc(v.tensor_sub(stt[:, 1024:1536], stt[:, 512:1024], stt[:, 1024:1536]),
                s_dve, 1)
        block.vector(f)
        var_rdy = s_dve.v

        def f(sc, vr=var_rdy):
            sc.wait_ge(s_dve.h, vr)
            inc(sc.activation(stt[:, 1536:2048], stt[:, 1024:1536], AF.Ln, bias=g_cst[:, 2:3]), s_act, 1)
            sc.wait_ge(s_act.h, s_act.v)
            inc(sc.activation(stt[:, 1536:2048], stt[:, 1536:2048], AF.Exp, scale=g_cst[:, 3:4]),
                s_act, 1)
        block.scalar(f)
        rstd_rdy = s_act.v

        x2_done = {}
        tp_free = [0, 0]
        ev_free = [0, 0, 0, 0]
        for dm in range(32):
            pb = dm % 2

            def f(s, dm=dm, pb=pb, sp=rspill[dm]):
                s.wait_ge(s_spill.h, sp)
                if dm >= 2:
                    s.wait_ge(s_pe.h, x2_done[dm - 2])
                inc(s.dma_start(out=rls[pb][:, :],
                                in_=bass.AP(sp_r, dm * 128 * 512, [[512, 128], [1, 512]])),
                    s_hdma, 16)
            block.sync(f)
            rl_rdy = s_hdma.v

            def f(v, pb=pb, rr=rl_rdy, sr=rstd_rdy):
                v.wait_ge(s_hdma.h, rr)
                v.wait_ge(s_act.h, sr)
                v.tensor_sub(rls[pb][:, :], rls[pb][:, :], stt[:, 0:512])
                inc(v.tensor_mul(rls[pb][:, :], rls[pb][:, :], stt[:, 1536:2048]), s_dve, 1)
            block.vector(f)
            x2_rdy = s_dve.v

            for j in range(4):
                tb = j % 2

                def f(t, pb=pb, j=j, tb=tb, xr=x2_rdy, tf=tp_free[tb]):
                    t.wait_ge(s_dve.h, xr)
                    if tf:
                        t.wait_ge(s_dve.h, tf)
                    inc(t.transpose(tps[tb][:, :], rls[pb][:, j * 128:(j + 1) * 128],
                                    g_idf[:, :]), s_pe, 1)
                block.tensor(f)
                tp_rdy = s_pe.v
                if j == 3:
                    x2_done[dm] = s_pe.v

                def f(v, tb=tb, j=j, tr=tp_rdy, ef=ev_free[j]):
                    v.wait_ge(s_pe.h, tr)
                    if ef:
                        v.wait_ge(s_out.h, ef)
                    inc(v.tensor_copy(evs[tb][:, j * 128:(j + 1) * 128], tps[tb][:, :]),
                        s_dve, 1)
                block.vector(f)
                evr = s_dve.v
                tp_free[tb] = s_dve.v

                def f(s, dm=dm, j=j, tb=tb, er=evr, B=B):
                    s.wait_ge(s_dve.h, er)
                    dst = bass.AP(d_out, (B * 512 + j * 128) * DIM + dm * 128,
                                  [[DIM, 128], [1, 128]])
                    inc(s.dma_start(out=dst, in_=evs[tb][:, j * 128:(j + 1) * 128]),
                        s_out, 16)
                block.sync(f)
                ev_free[j] = s_out.v
        barrier()
        st4.close()
        st3o.close()
    def f(s):
        s.wait_ge(s_out.h, s_out.v)
    block.sync(f)

    blk_cm.__exit__(None, None, None)
    es.close()
    return nc


def kernel(L0_attn_hidden_states, L0_attn_in_proj_w, L0_attn_conv_w, L0_attn_conv_b,
           L0_attn_dt_bias, L0_attn_A_log, L0_attn_D, L0_attn_norm_w,
           L0_attn_out_proj_w, L0_ffn_weight, L0_ffn_bias, L0_ffn_lora_A,
           L0_ffn_lora_B, L0_ffn_lora_start_indices):
    X = np.ascontiguousarray(np.asarray(L0_attn_hidden_states, np.float32))
    Win = np.asarray(L0_attn_in_proj_w, np.float32)
    Wout = np.asarray(L0_attn_out_proj_w, np.float32)
    Wffn = np.asarray(L0_ffn_weight, np.float32)

    order = np.concatenate([np.arange(INTER, INTER + CONV_DIM),
                            np.arange(INTER + CONV_DIM, NPROJ),
                            np.arange(0, INTER)])
    Wsel = Win.T[:, order]
    win_t = np.ascontiguousarray(Wsel.reshape(NKT, 128, NTILE, 128).transpose(2, 1, 0, 3))
    wout_t = np.ascontiguousarray(Wout.T.reshape(64, 128, 32, 128).transpose(2, 1, 0, 3))
    wffn_t = np.ascontiguousarray(Wffn.T.reshape(NKT, 128, 32, 128).transpose(2, 1, 0, 3))

    cw_t = np.ascontiguousarray(
        np.asarray(L0_attn_conv_w, np.float32).reshape(N_HBC_T, 128, 4)
        .transpose(1, 0, 2).reshape(128, N_HBC_T * 4))
    cb_t = np.ascontiguousarray(np.asarray(L0_attn_conv_b, np.float32).reshape(N_HBC_T, 128).T)
    nw_t = np.ascontiguousarray(np.asarray(L0_attn_norm_w, np.float32).reshape(64, 128).T)
    fb_t = np.ascontiguousarray(np.asarray(L0_ffn_bias, np.float32).reshape(32, 128).T)
    dtb = np.ascontiguousarray(np.asarray(L0_attn_dt_bias, np.float32).reshape(128, 1))
    alog = np.ascontiguousarray(np.asarray(L0_attn_A_log, np.float32).reshape(128, 1))
    D = np.asarray(L0_attn_D, np.float32)
    dbc = np.ascontiguousarray(np.tile(D[None, :], (128, 1)))
    mskt = np.ascontiguousarray(np.triu(np.ones((128, 128), np.float32)))
    idf = np.eye(128, dtype=np.float32)
    onb = np.ones((128, 128), BFNP)
    onf = np.ones((1, 128), np.float32)
    idb = np.eye(128, dtype=BFNP)
    mneg = np.ascontiguousarray(np.tile(np.tril(np.full((128, 128), -1e9, np.float32), -1),
                                        (1, 4)).astype(BFNP))
    laT = np.ascontiguousarray(np.asarray(L0_ffn_lora_A, np.float32))
    lbT = np.ascontiguousarray(np.asarray(L0_ffn_lora_B, np.float32))
    laT = np.ascontiguousarray(np.asarray(L0_ffn_lora_A, np.float32).transpose(0, 2, 1))
    lbT = np.ascontiguousarray(np.asarray(L0_ffn_lora_B, np.float32).transpose(0, 2, 1))
    lsi = np.ascontiguousarray(np.asarray(L0_ffn_lora_start_indices, np.int32).reshape(1, 4))

    nc = build_program()
    in_maps = _in_maps_from(X, win_t, wout_t, wffn_t, cw_t, cb_t, nw_t, fb_t, dtb,
                            alog, dbc, mskt, idf, onb, onf, laT, lbT, lsi, idb, mneg)
    res = run_bass_kernel_spmd(nc, in_maps, core_ids=list(range(8)))
    out = np.empty((4, 4096, DIM), np.float32)
    for c in range(8):
        b, half = c // 2, c % 2
        out[b, half * TOK:(half + 1) * TOK, :] = res.results[c]["out"]
    return out


def host_prep(inputs):
    import inspect
    sig = inspect.signature(kernel)
    kw = {k: inputs[k] for k in sig.parameters}
    L = kw
    X = np.ascontiguousarray(np.asarray(L["L0_attn_hidden_states"], np.float32))
    Win = np.asarray(L["L0_attn_in_proj_w"], np.float32)
    Wout = np.asarray(L["L0_attn_out_proj_w"], np.float32)
    Wffn = np.asarray(L["L0_ffn_weight"], np.float32)
    order = np.concatenate([np.arange(INTER, INTER + CONV_DIM),
                            np.arange(INTER + CONV_DIM, NPROJ),
                            np.arange(0, INTER)])
    Wsel = Win.T[:, order]
    win_t = np.ascontiguousarray(Wsel.reshape(NKT, 128, NTILE, 128).transpose(2, 1, 0, 3))
    wout_t = np.ascontiguousarray(Wout.T.reshape(64, 128, 32, 128).transpose(2, 1, 0, 3))
    wffn_t = np.ascontiguousarray(Wffn.T.reshape(NKT, 128, 32, 128).transpose(2, 1, 0, 3))
    cw_t = np.ascontiguousarray(
        np.asarray(L["L0_attn_conv_w"], np.float32).reshape(N_HBC_T, 128, 4)
        .transpose(1, 0, 2).reshape(128, N_HBC_T * 4))
    cb_t = np.ascontiguousarray(np.asarray(L["L0_attn_conv_b"], np.float32).reshape(N_HBC_T, 128).T)
    nw_t = np.ascontiguousarray(np.asarray(L["L0_attn_norm_w"], np.float32).reshape(64, 128).T)
    fb_t = np.ascontiguousarray(np.asarray(L["L0_ffn_bias"], np.float32).reshape(32, 128).T)
    dtb = np.ascontiguousarray(np.asarray(L["L0_attn_dt_bias"], np.float32).reshape(128, 1))
    alog = np.ascontiguousarray(np.asarray(L["L0_attn_A_log"], np.float32).reshape(128, 1))
    D = np.asarray(L["L0_attn_D"], np.float32)
    dbc = np.ascontiguousarray(np.tile(D[None, :], (128, 1)))
    mskt = np.ascontiguousarray(np.triu(np.ones((128, 128), np.float32)))
    idf = np.eye(128, dtype=np.float32)
    onb = np.ones((128, 128), BFNP)
    onf = np.ones((1, 128), np.float32)
    idb = np.eye(128, dtype=BFNP)
    mneg = np.ascontiguousarray(np.tile(np.tril(np.full((128, 128), -1e9, np.float32), -1),
                                        (1, 4)).astype(BFNP))
    laT = np.ascontiguousarray(np.asarray(L["L0_ffn_lora_A"], np.float32).transpose(0, 2, 1))
    lbT = np.ascontiguousarray(np.asarray(L["L0_ffn_lora_B"], np.float32).transpose(0, 2, 1))
    lsi = np.ascontiguousarray(np.asarray(L["L0_ffn_lora_start_indices"], np.int32).reshape(1, 4))
    return _in_maps_from(X, win_t, wout_t, wffn_t, cw_t, cb_t, nw_t, fb_t, dtb,
                         alog, dbc, mskt, idf, onb, onf, laT, lbT, lsi, idb, mneg)


def _in_maps_from(X, win_t, wout_t, wffn_t, cw_t, cb_t, nw_t, fb_t, dtb,
                  alog, dbc, mskt, idf, onb, onf, laT, lbT, lsi, idb, mneg):
    in_maps = []
    for c in range(8):
        b, half = c // 2, c % 2
        toks = X[b, half * TOK:(half + 1) * TOK, :]
        halo = np.zeros((3, DIM), np.float32) if half == 0 else X[b, TOK - 3:TOK, :]
        xt = np.ascontiguousarray(np.concatenate([halo, toks], 0).T)
        cst = np.zeros((128, 8), np.float32)
        cst[:, 0] = float(b)
        cst[:, 1] = float(half)
        cst[:, 2] = EPS
        cst[:, 3] = -0.5
        cst[:, 4] = 1.0 / INTER
        in_maps.append({
            "xT": xt, "w_in": win_t, "w_out": wout_t, "w_ffn": wffn_t,
            "cw": cw_t, "cb": cb_t, "nw": nw_t, "fb": fb_t, "dtb": dtb,
            "alog": alog, "dbc": dbc, "mskt": mskt, "cst": cst,
            "idf": idf, "onb": onb, "onf": onf, "idb": idb, "mneg": mneg,
            "laT": laT, "lbT": lbT, "lsi": lsi,
        })
    return in_maps


# version 06
# speedup vs baseline: 1.1568x; 1.1568x over previous
import numpy as np
import ml_dtypes
import concourse.bass as bass
import concourse.mybir as mybir
from concourse.bass_utils import run_bass_kernel_spmd

F32 = mybir.dt.float32
BF16 = mybir.dt.bfloat16
I32 = mybir.dt.int32
AF = mybir.ActivationFunctionType
ALU = mybir.AluOpType
BFNP = ml_dtypes.bfloat16
AX = mybir.AxisListType.X

H, P, G, N, CS, KCONV, DIM = 128, 64, 8, 128, 128, 4, 4096
INTER = H * P
CONV_DIM = INTER + 2 * G * N
NPROJ = INTER + CONV_DIM + H
EPS = 1e-5
TOK = 2048
NCHUNK = TOK // CS
NKT = DIM // 128
NTILE = NPROJ // 128
N_HBC_T = CONV_DIM // 128
T_DT = N_HBC_T
T_GATE0 = N_HBC_T + 1


class Sem:
    def __init__(self, h):
        self.h = h
        self.v = 0


def inc(instr, s, n=1):
    instr.then_inc(s.h, n)
    s.v += n
    return instr


def build_program(taps=(), flags=()):
    nc = bass.Bass("TRN2", target_bir_lowering=False)

    d_xT = nc.dram_tensor("xT", [DIM, TOK + 3], F32, kind="ExternalInput")
    d_win = nc.dram_tensor("w_in", [NTILE, 128, NKT, 128], F32, kind="ExternalInput")
    d_wout = nc.dram_tensor("w_out", [32, 128, 64, 128], F32, kind="ExternalInput")
    d_wffn = nc.dram_tensor("w_ffn", [32, 128, NKT, 128], F32, kind="ExternalInput")
    d_cw = nc.dram_tensor("cw", [128, N_HBC_T * 4], F32, kind="ExternalInput")
    d_cb = nc.dram_tensor("cb", [128, N_HBC_T], F32, kind="ExternalInput")
    d_nw = nc.dram_tensor("nw", [128, 64], F32, kind="ExternalInput")
    d_fb = nc.dram_tensor("fb", [128, 32], F32, kind="ExternalInput")
    d_dtb = nc.dram_tensor("dtb", [128, 1], F32, kind="ExternalInput")
    d_alog = nc.dram_tensor("alog", [128, 1], F32, kind="ExternalInput")
    d_dbc = nc.dram_tensor("dbc", [128, 128], F32, kind="ExternalInput")
    d_msk = nc.dram_tensor("mskt", [128, 128], F32, kind="ExternalInput")
    d_cst = nc.dram_tensor("cst", [128, 8], F32, kind="ExternalInput")
    d_idf = nc.dram_tensor("idf", [128, 128], F32, kind="ExternalInput")
    d_onb = nc.dram_tensor("onb", [128, 128], BF16, kind="ExternalInput")
    d_idb = nc.dram_tensor("idb", [128, 128], BF16, kind="ExternalInput")
    d_mneg = nc.dram_tensor("mneg", [128, 512], BF16, kind="ExternalInput")
    d_onf = nc.dram_tensor("onf", [1, 128], F32, kind="ExternalInput")
    d_laT = nc.dram_tensor("laT", [3, DIM, 16], F32, kind="ExternalInput")
    d_lbT = nc.dram_tensor("lbT", [3, 16, DIM], F32, kind="ExternalInput")
    d_lsi = nc.dram_tensor("lsi", [1, 4], I32, kind="ExternalInput")
    d_out = nc.dram_tensor("out", [TOK, DIM], F32, kind="ExternalOutput")
    sp_hbc = nc.dram_tensor("sp_hbc", [CONV_DIM, TOK], BF16, kind="ExternalOutput" if "sp_hbc" in taps else "Internal")
    sp_gate = nc.dram_tensor("sp_gate", [INTER, TOK], BF16, kind="ExternalOutput" if "sp_gate" in taps else "Internal")
    sp_dt = nc.dram_tensor("sp_dt", [128, TOK], F32, kind="ExternalOutput" if "sp_dt" in taps else "Internal")
    sp_cumA = nc.dram_tensor("sp_cumA", [128, TOK], F32, kind="ExternalOutput" if "sp_cumA" in taps else "Internal")
    sp_z = nc.dram_tensor("sp_z", [TOK, INTER], BF16, kind="ExternalOutput" if "sp_z" in taps else "Internal")
    sp_r = nc.dram_tensor("sp_r", [DIM, 512], F32, kind="ExternalOutput" if "sp_r" in taps else "Internal")
    sp_x1 = nc.dram_tensor("sp_x1", [DIM, TOK], F32, kind="ExternalOutput" if "sp_x1" in taps else "Internal")
    d_dbg0 = nc.dram_tensor("dbg0", [128, 64], F32, kind="ExternalOutput" if "dbg0" in taps else "Internal")
    d_dbga = nc.dram_tensor("dbga", [128, 512], BF16, kind="ExternalOutput" if "dbg0" in taps else "Internal")
    cc_in = nc.dram_tensor("cc_in", [128, INTER], F32)
    cc_out = nc.dram_tensor("cc_out", [256, INTER], F32)

    from contextlib import ExitStack
    es = ExitStack()

    def SB(name, shape, dtype):
        return es.enter_context(nc.sbuf_tensor(name, shape, dtype))

    def sem(name):
        return Sem(es.enter_context(nc.semaphore(name)))

    g_fb = SB("g_fb", [128, 32], F32)
    g_dtb = SB("g_dtb", [128, 1], F32)
    g_alog = SB("g_alog", [128, 1], F32)
    g_acol = SB("g_acol", [128, 1], F32)
    g_dbc = SB("g_dbc", [128, 128], BF16)
    g_msk = SB("g_msk", [128, 128], F32)
    g_cst = SB("g_cst", [128, 8], F32)
    g_idf = SB("g_idf", [128, 128], F32)
    g_onb = SB("g_onb", [128, 128], BF16)
    g_idb = SB("g_idb", [128, 128], BF16)
    g_mneg = SB("g_mneg", [128, 512], BF16)
    g_onf = SB("g_onf", [1, 128], F32)
    g_aeff = SB("g_aeff", [128, 512], BF16)
    g_beff = SB("g_beff", [16, DIM], BF16)

    s_wld = sem("s_wld")
    s_hdma = sem("s_hdma")
    s_spill = sem("s_spill")
    s_out = sem("s_out")
    s_pe = sem("s_pe")
    s_dve = sem("s_dve")
    s_act = sem("s_act")
    s_pool = sem("s_pool")
    s_cc = sem("s_cc")
    s_bar = sem("s_bar")

    blk_cm = nc.Block()
    block = blk_cm.__enter__()
    nbar = [0]

    def barrier():
        nbar[0] += 1
        tgt = 5 * nbar[0]
        for e in ("tensor", "vector", "scalar", "gpsimd", "sync"):
            def f(eng, e=e):
                if e in ("vector", "scalar", "gpsimd"):
                    eng.drain().then_inc(s_bar.h, 1)
                else:
                    eng.sem_inc(s_bar.h, 1)
                eng.wait_ge(s_bar.h, tgt)
            getattr(block, e)(f)
        s_bar.v = tgt

    st0 = ExitStack()
    laT = st0.enter_context(nc.sbuf_tensor("p0_laT", [128, 3 * NKT * 16], F32))
    lbT = st0.enter_context(nc.sbuf_tensor("p0_lbT", [16, 3 * DIM], F32))
    p0m = st0.enter_context(nc.sbuf_tensor("p0_m", [128, 64], F32))
    lsi_i = st0.enter_context(nc.sbuf_tensor("p0_lsi", [1, 4], I32))
    tmpA = st0.enter_context(nc.sbuf_tensor("p0_tA", [128, 512], F32))
    tmpB = st0.enter_context(nc.sbuf_tensor("p0_tB", [16, DIM], F32))
    ps0 = st0.enter_context(nc.psum_tensor("p0_ps", [128, 512], F32))

    def f(g):
        for o, i in [
            (g_fb[:, :], d_fb.ap()),
            (g_dtb[:, :], d_dtb.ap()), (g_alog[:, :], d_alog.ap()),
            (g_dbc[:, :], d_dbc.ap()), (g_msk[:, :], d_msk.ap()),
            (g_cst[:, :], d_cst.ap()), (g_idf[:, :], d_idf.ap()),
            (g_onb[:, :], d_onb.ap()), (g_onf[:, :], d_onf.ap()),
            (g_idb[:, :], d_idb.ap()), (g_mneg[:, :], d_mneg.ap()),
            (lsi_i[:, :], d_lsi.ap()),
            (laT[:, :], bass.AP(d_laT, 0, [[16, 128], [DIM * 16, 3], [128 * 16, NKT], [1, 16]])),
            (lbT[:, :], bass.AP(d_lbT, 0, [[DIM, 16], [16 * DIM, 3], [1, DIM]])),
        ]:
            inc(g.dma_start(out=o, in_=i), s_wld, 16)
    block.gpsimd(f)
    n_p0 = s_wld.v

    def f(v):
        v.wait_ge(s_wld.h, n_p0)
        inc(v.tensor_copy(p0m[:1, 0:4], lsi_i[:, :]), s_dve, 1)
    block.vector(f)
    v_lsi = s_dve.v

    def f(t):
        t.wait_ge(s_dve.h, v_lsi)
        inc(t.matmul(ps0[:, 0:4], g_onf[:1, :], p0m[:1, 0:4], start=True, stop=True), s_pe, 1)
    block.tensor(f)
    pe_lsi = s_pe.v

    def f(v):
        v.wait_ge(s_pe.h, pe_lsi)

        def step(ins):
            inc(ins, s_dve, 1)
            v.wait_ge(s_dve.h, s_dve.v)
        step(v.tensor_scalar(p0m[:, 0:4], ps0[:, 0:4], g_cst[:, 0:1], None, ALU.is_le))
        step(v.reduce_sum(p0m[:, 4:5], p0m[:, 0:4], axis=AX))
        step(v.tensor_scalar(p0m[:, 5:6], p0m[:, 4:5], -1.0, None, ALU.add))
        step(v.tensor_scalar(p0m[:, 6:7], p0m[:, 5:6], 1.0, None, ALU.is_ge))
        step(v.tensor_scalar(p0m[:, 7:8], p0m[:, 5:6], -1.0, None, ALU.add))
        step(v.tensor_scalar(p0m[:, 7:8], p0m[:, 7:8], 0.0, None, ALU.max))
        step(v.tensor_scalar(p0m[:, 8:9], p0m[:, 7:8], 2.0, None, ALU.min))
        for k in range(3):
            step(v.tensor_scalar(p0m[:, 9 + k:10 + k], p0m[:, 8:9], float(k), None, ALU.is_equal))
            step(v.tensor_tensor(p0m[:, 32 + k:33 + k], p0m[:, 9 + k:10 + k],
                                 p0m[:, 6:7], ALU.mult))
        wc = p0m[:, 32:35]
        step(v.tensor_scalar(tmpA[:, :], laT[:, 0:512], wc[:, 0:1], None, ALU.mult))
        step(v.scalar_tensor_tensor(tmpA[:, :], laT[:, 512:1024], wc[:, 1:2], tmpA[:, :], ALU.mult, ALU.add))
        step(v.scalar_tensor_tensor(tmpA[:, :], laT[:, 1024:1536], wc[:, 2:3], tmpA[:, :], ALU.mult, ALU.add))
        step(v.tensor_copy(g_aeff[:, :], tmpA[:, :]))
        step(v.tensor_scalar(tmpB[:, :], lbT[:, 0:DIM], wc[:16, 0:1], None, ALU.mult))
        step(v.scalar_tensor_tensor(tmpB[:, :], lbT[:, DIM:2 * DIM], wc[:16, 1:2], tmpB[:, :], ALU.mult, ALU.add))
        step(v.scalar_tensor_tensor(tmpB[:, :], lbT[:, 2 * DIM:3 * DIM], wc[:16, 2:3], tmpB[:, :], ALU.mult, ALU.add))
        step(v.tensor_copy(g_beff[:, :], tmpB[:, :]))
    block.vector(f)

    def f(sc):
        sc.wait_ge(s_wld.h, n_p0)
        inc(sc.activation(g_acol[:, :], g_alog[:, :], AF.Exp), s_act, 1)
    block.scalar(f)
    a_exp = s_act.v

    def f(v):
        v.wait_ge(s_act.h, a_exp)
        inc(v.tensor_scalar(g_acol[:, :], g_acol[:, :], -1.0, None, ALU.mult), s_dve, 1)
    block.vector(f)

    if "dbg0" in taps:
        dv = s_dve.v

        def f(s, dv=dv):
            s.wait_ge(s_dve.h, dv)
            inc(s.dma_start(out=d_dbg0.ap(), in_=p0m[:, :]), s_spill, 16)
            inc(s.dma_start(out=d_dbga.ap(), in_=g_aeff[:, :]), s_spill, 16)
            s.wait_ge(s_spill.h, s_spill.v)
        block.sync(f)
    barrier()
    st0.close()
    st1 = ExitStack()
    xbT = st1.enter_context(nc.sbuf_tensor("p1_xbT", [128, NKT * (TOK + 3)], BF16))
    p1w0 = st1.enter_context(nc.sbuf_tensor("p1_w0", [128, NKT * 128], BF16))
    p1w1 = st1.enter_context(nc.sbuf_tensor("p1_w1", [128, NKT * 128], BF16))
    ci0 = st1.enter_context(nc.sbuf_tensor("p1_ci0", [128, TOK + 3], BF16))
    ci1 = st1.enter_context(nc.sbuf_tensor("p1_ci1", [128, TOK + 3], BF16))
    pa0 = st1.enter_context(nc.sbuf_tensor("p1_a0", [128, TOK], F32))
    pa1 = st1.enter_context(nc.sbuf_tensor("p1_a1", [128, TOK], F32))
    ph0 = st1.enter_context(nc.sbuf_tensor("p1_h0", [128, TOK], BF16))
    ph1 = st1.enter_context(nc.sbuf_tensor("p1_h1", [128, TOK], BF16))
    pg0 = st1.enter_context(nc.sbuf_tensor("p1_g0", [128, TOK], BF16))
    pg1 = st1.enter_context(nc.sbuf_tensor("p1_g1", [128, TOK], BF16))
    psA = st1.enter_context(nc.psum_tensor("p1_psA", [128, 512], F32))
    psB = st1.enter_context(nc.psum_tensor("p1_psB", [128, 512], F32))
    psH = st1.enter_context(nc.psum_tensor("p1_psH", [128, 512], F32))

    g_cw = st1.enter_context(nc.sbuf_tensor("g_cw", [128, N_HBC_T * 4], F32))
    g_cb = st1.enter_context(nc.sbuf_tensor("g_cb", [128, N_HBC_T], F32))
    g_nw = st1.enter_context(nc.sbuf_tensor("g_nw", [128, 64], F32))
    wbufs, cins, accs = (p1w0, p1w1), (ci0, ci1), (pa0, pa1)
    hbufs, gbufs, slots = (ph0, ph1), (pg0, pg1), (psA, psB)

    def f(g):
        for o, i in [(g_cw[:, :], d_cw.ap()), (g_cb[:, :], d_cb.ap()),
                     (g_nw[:, :], d_nw.ap())]:
            inc(g.dma_start(out=o, in_=i), s_wld, 16)
        for i in range(4):
            src = bass.AP(d_xT, i * 8 * 128 * (TOK + 3),
                          [[TOK + 3, 128], [128 * (TOK + 3), 8], [1, TOK + 3]])
            dst = bass.AP(xbT, i * 8 * (TOK + 3),
                          [[NKT * (TOK + 3), 128], [TOK + 3, 8], [1, TOK + 3]])
            inc(g.dma_start(out=dst, in_=src), s_wld, 16)
    block.gpsimd(f)
    x_ld = s_wld.v

    slot_free = [0, 0]
    halo_free = 0
    pe_tile_end = []
    conv_done = {}
    silu_done = {}
    spill_done = {}

    def tile_dst(ti):
        if ti < 64:
            return sp_hbc, ti * 128
        if ti < 72:
            return sp_hbc, INTER + (ti - 64) * 128
        if ti < 80:
            return sp_hbc, INTER + G * N + (ti - 72) * 128
        if ti == T_DT:
            return sp_dt, 0
        return sp_gate, (ti - T_GATE0) * 128

    for ti in range(NTILE):
        is_hbc = ti < N_HBC_T
        is_gate = ti >= T_GATE0
        pb = ti % 2

        def f(g, ti=ti, pb=pb):
            if ti >= 2:
                g.wait_ge(s_pe.h, pe_tile_end[ti - 2])
            src = bass.AP(d_win, ti * 128 * NKT * 128, [[NKT * 128, 128], [1, NKT * 128]])
            inc(g.dma_start(out=wbufs[pb][:, :], in_=src), s_wld, 16)
        block.gpsimd(f)
        w_rdy = s_wld.v

        for nb in range(4):
            sl = nb % 2

            def f(t, pb=pb, nb=nb, sl=sl, wr=w_rdy, sf=slot_free[sl]):
                t.wait_ge(s_wld.h, max(wr, x_ld))
                if sf:
                    t.wait_ge(s_act.h, sf)
                for kt in range(NKT):
                    ins = t.matmul(slots[sl][:, :],
                                   wbufs[pb][:, kt * 128:(kt + 1) * 128],
                                   xbT[:, kt * (TOK + 3) + 3 + nb * 512:
                                       kt * (TOK + 3) + 3 + (nb + 1) * 512],
                                   start=(kt == 0), stop=(kt == NKT - 1))
                    if kt == NKT - 1:
                        inc(ins, s_pe, 1)
            block.tensor(f)
            nb_pe = s_pe.v

            def f(sc, ti=ti, pb=pb, nb=nb, sl=sl, npe=nb_pe,
                  is_hbc=is_hbc, is_gate=is_gate):
                if is_hbc:
                    if nb == 0 and ti >= 2:
                        sc.wait_ge(s_dve.h, conv_done[ti - 2])
                elif is_gate:
                    if nb == 0 and ti >= 2 and (ti - 2) >= T_GATE0:
                        sc.wait_ge(s_spill.h, spill_done[ti - 2])
                else:
                    if nb == 0:
                        sc.wait_ge(s_act.h, silu_done[N_HBC_T - 1])
                sc.wait_ge(s_pe.h, npe)
                if is_hbc:
                    ins = sc.activation(cins[pb][:, 3 + nb * 512:3 + (nb + 1) * 512],
                                        slots[sl][:, :], AF.Copy)
                elif is_gate:
                    ins = sc.activation(gbufs[pb][:, nb * 512:(nb + 1) * 512],
                                        slots[sl][:, :], AF.Silu)
                else:
                    ins = sc.activation(accs[0][:, nb * 512:(nb + 1) * 512],
                                        slots[sl][:, :], AF.Copy)
                inc(ins, s_act, 1)
            block.scalar(f)
            slot_free[sl] = s_act.v

        if is_hbc:
            def f(t, pb=pb, hf=halo_free):
                if hf:
                    t.wait_ge(s_act.h, hf)
                for kt in range(NKT):
                    ins = t.matmul(psH[:, 0:3],
                                   wbufs[pb][:, kt * 128:(kt + 1) * 128],
                                   xbT[:, kt * (TOK + 3): kt * (TOK + 3) + 3],
                                   start=(kt == 0), stop=(kt == NKT - 1))
                    if kt == NKT - 1:
                        inc(ins, s_pe, 1)
            block.tensor(f)
            halo_pe = s_pe.v

            def f(sc, pb=pb, hp=halo_pe):
                sc.wait_ge(s_pe.h, hp)
                inc(sc.activation(cins[pb][:, 0:3], psH[:, 0:3], AF.Copy), s_act, 1)
            block.scalar(f)
            halo_free = s_act.v
            evac_done = s_act.v

            def f(v, ti=ti, pb=pb, ed=evac_done):
                v.wait_ge(s_act.h, ed)
                if ti >= 1 and (ti - 1) in silu_done:
                    v.wait_ge(s_act.h, silu_done[ti - 1])
                ci = cins[pb]
                a_, b_ = accs[pb], accs[1 - pb]
                v.tensor_scalar(a_[:, :], ci[:, 0:TOK], g_cw[:, ti * 4:ti * 4 + 1],
                                g_cb[:, ti:ti + 1], ALU.mult, ALU.add)
                v.scalar_tensor_tensor(b_[:, :], ci[:, 1:1 + TOK], g_cw[:, ti * 4 + 1:ti * 4 + 2],
                                       a_[:, :], ALU.mult, ALU.add)
                v.scalar_tensor_tensor(a_[:, :], ci[:, 2:2 + TOK], g_cw[:, ti * 4 + 2:ti * 4 + 3],
                                       b_[:, :], ALU.mult, ALU.add)
                ins = v.scalar_tensor_tensor(b_[:, :], ci[:, 3:3 + TOK], g_cw[:, ti * 4 + 3:ti * 4 + 4],
                                             a_[:, :], ALU.mult, ALU.add)
                inc(ins, s_dve, 1)
            block.vector(f)
            conv_done[ti] = s_dve.v

            def f(sc, ti=ti, pb=pb, cd=conv_done[ti]):
                sc.wait_ge(s_dve.h, cd)
                if ti >= 2 and (ti - 2) in spill_done:
                    sc.wait_ge(s_spill.h, spill_done[ti - 2])
                inc(sc.activation(hbufs[pb][:, :], accs[1 - pb][:, :], AF.Silu), s_act, 1)
            block.scalar(f)
            silu_done[ti] = s_act.v

            def f(s, ti=ti, pb=pb, sd=silu_done[ti]):
                s.wait_ge(s_act.h, sd)
                dst, r0 = tile_dst(ti)
                inc(s.dma_start(out=bass.AP(dst, r0 * TOK, [[TOK, 128], [1, TOK]]),
                                in_=hbufs[pb][:, :]), s_spill, 16)
            block.sync(f)
            spill_done[ti] = s_spill.v
        elif is_gate:
            gs_done = s_act.v

            def f(v, ti=ti, pb=pb, gd=gs_done):
                v.wait_ge(s_act.h, gd)
                gt = ti - T_GATE0
                ins = v.tensor_scalar(gbufs[pb][:, :], gbufs[pb][:, :],
                                      g_nw[:, gt:gt + 1], None, ALU.mult)
                inc(ins, s_dve, 1)
            block.vector(f)
            gn = s_dve.v

            def f(s, ti=ti, pb=pb, gn=gn):
                s.wait_ge(s_dve.h, gn)
                dst, r0 = tile_dst(ti)
                inc(s.dma_start(out=bass.AP(dst, r0 * TOK, [[TOK, 128], [1, TOK]]),
                                in_=gbufs[pb][:, :]), s_spill, 16)
            block.sync(f)
            spill_done[ti] = s_spill.v
        else:
            dt_done = s_act.v

            def f(s, de=dt_done):
                s.wait_ge(s_act.h, de)
                inc(s.dma_start(out=sp_dt.ap(), in_=accs[0][:, :]), s_spill, 16)
            block.sync(f)
            spill_done[ti] = s_spill.v

        pe_tile_end.append(s_pe.v)

    def f(s):
        s.wait_ge(s_spill.h, s_spill.v)
    block.sync(f)
    barrier()
    st1.close()
    st2 = ExitStack()
    S = st2.enter_context(nc.sbuf_tensor("p2_S", [128, INTER], F32))
    Sb = st2.enter_context(nc.sbuf_tensor("p2_Sb", [128, INTER], BF16))
    dts = st2.enter_context(nc.sbuf_tensor("p2_dt", [128, TOK], F32))
    cumA = st2.enter_context(nc.sbuf_tensor("p2_cA", [128, TOK], F32))
    Ftl = st2.enter_context(nc.sbuf_tensor("p2_F", [128, 128 * 8], F32))
    cl1p = st2.enter_context(nc.sbuf_tensor("p2_c1", [1, 128], F32))
    bcA = st2.enter_context(nc.psum_tensor("p2_bcA", [128, 512], F32))
    bcB = st2.enter_context(nc.psum_tensor("p2_bcB", [128, 512], F32))
    gmps = st2.enter_context(nc.psum_tensor("p2_gm", [128, 1024], F32))
    ydA = st2.enter_context(nc.psum_tensor("p2_ydA", [128, 512], F32))
    ydB = st2.enter_context(nc.psum_tensor("p2_ydB", [128, 512], F32))
    yoA = st2.enter_context(nc.psum_tensor("p2_yoA", [128, 512], F32))
    yoB = st2.enter_context(nc.psum_tensor("p2_yoB", [128, 512], F32))

    F3T_, F2T_, F5T_, E3_, F2_, F5_, DEC_, F3_ = range(8)
    FP = 128 * 8

    def Fv(i):
        return Ftl[:, i * 128:(i + 1) * 128]

    yd_ps, yo_ps, bc_ps = (ydA, ydB), (yoA, yoB), (bcA, bcB)

    war = {"e3_dve": 0, "dec_dve": 0, "f3t_pool": 0, "f5t_act": 0,
           "ftr_pe": 0, "bcA_dve": 0, "bcB_act": 0, "cl1p_pe": 0,
           "gmm_pool": 0}
    st_free = [0, 0]

    st2a = ExitStack()
    xa0 = st2a.enter_context(nc.sbuf_tensor("a_x0", [128, 9216], BF16))
    xa1 = st2a.enter_context(nc.sbuf_tensor("a_x1", [128, 9216], BF16))
    ab0 = st2a.enter_context(nc.sbuf_tensor("a_b0", [128, 512], BF16))
    ab1 = st2a.enter_context(nc.sbuf_tensor("a_b1", [128, 512], BF16))
    zz = st2a.enter_context(nc.sbuf_tensor("p2_zz", [128, 128], F32))
    adtc = st2a.enter_context(nc.sbuf_tensor("p2_ad", [128, 128], F32))
    xas = (xa0, xa1)
    bds = (ab0, ab1)

    def f(s):
        inc(s.dma_start(out=dts[:, :], in_=sp_dt.ap()), s_hdma, 16)
    block.sync(f)
    dt_ld = s_hdma.v

    def f(v, dl=dt_ld):
        v.wait_ge(s_hdma.h, dl)
        v.memset(zz[:, :], 0.0)
        inc(v.tensor_scalar(dts[:, :], dts[:, :], g_dtb[:, 0:1], None, ALU.add), s_dve, 1)
    block.vector(f)
    v_abs = s_dve.v

    def f(sc, va=v_abs):
        sc.wait_ge(s_dve.h, va)
        sc.activation(cumA[:, :], dts[:, :], AF.Abs)
        sc.activation(cumA[:, :], cumA[:, :], AF.Exp, scale=-1.0)
        inc(sc.activation(cumA[:, :], cumA[:, :], AF.Ln, bias=1.0), s_act, 1)
    block.scalar(f)
    a_sp = s_act.v

    def f(v, asp=a_sp):
        v.wait_ge(s_act.h, asp)
        v.tensor_scalar(dts[:, :], dts[:, :], 0.0, None, ALU.max)
        v.tensor_add(dts[:, :], dts[:, :], cumA[:, :])
        inc(v.memset(S[:, :], 0.0), s_dve, 1)
    block.vector(f)

    xa_free = [0, 0]
    bd_free = [0, 0]

    for c in range(NCHUNK):
        pb = c % 2
        csl = slice(c * 128, (c + 1) * 128)

        def f(s, c=c, pb=pb, xf=xa_free[pb]):
            if xf:
                s.wait_ge(s_pe.h, xf)
            inc(s.dma_start(out=xas[pb][:, :],
                            in_=bass.AP(sp_hbc, c * 128, [[TOK, 9216], [1, 128]]),
                            transpose=True), s_hdma, 16)
        block.sync(f)
        x_rdy = s_hdma.v

        def f(v, c=c, csl=csl):
            v.tensor_scalar(adtc[:, :], dts[:, csl], g_acol[:, 0:1], None, ALU.mult)
            ins = v.tensor_tensor_scan(cumA[:, csl], adtc[:, :], zz[:, :],
                                       0.0, ALU.add, ALU.add)
            inc(ins, s_dve, 1)
        block.vector(f)
        cum_rdy = s_dve.v

        def f(s, c=c, csl=csl, cr=cum_rdy, clf=war["cl1p_pe"]):
            s.wait_ge(s_dve.h, cr)
            inc(s.dma_start(out=bass.AP(sp_cumA, c * 128, [[TOK, 128], [1, 128]]),
                            in_=cumA[:, csl]), s_spill, 16)
            s.wait_ge(s_spill.h, s_spill.v)
            if clf:
                s.wait_ge(s_pe.h, clf)
            inc(s.dma_start(out=cl1p[:, :],
                            in_=bass.AP(sp_cumA, c * 128 + 127, [[0, 1], [TOK, 128], [1, 1]]),
                            allow_slow_non_contiguous=True), s_hdma, 16)
        block.sync(f)
        cl_rdy = s_hdma.v

        def f(sc, c=c, csl=csl, cr=cum_rdy, w1=war["e3_dve"]):
            sc.wait_ge(s_dve.h, cr)
            if w1:
                sc.wait_ge(s_dve.h, w1)
            ins = sc.activation(Fv(E3_), cumA[:, csl], AF.Exp, scale=-1.0,
                                bias=cumA[:, c * 128 + 127:c * 128 + 128])
            inc(ins, s_act, 1)
        block.scalar(f)
        e3_rdy = s_act.v

        def f(v, csl=csl, er=e3_rdy, w1=war["ftr_pe"]):
            v.wait_ge(s_act.h, er)
            if w1:
                v.wait_ge(s_pe.h, w1)
            inc(v.tensor_mul(Fv(F3_), Fv(E3_), dts[:, csl]), s_dve, 1)
        block.vector(f)
        f3_rdy = s_dve.v
        war["e3_dve"] = s_dve.v

        def f(t, fr=f3_rdy, clr=cl_rdy, wA=war["bcA_dve"], wB=war["bcB_act"]):
            if wA:
                t.wait_ge(s_dve.h, wA)
            t.wait_ge(s_dve.h, fr)
            inc(t.transpose(bcA[:, 0:128], Fv(F3_), g_idf[:, :]), s_pe, 1)
            if wB:
                t.wait_ge(s_act.h, wB)
            t.wait_ge(s_hdma.h, clr)
            inc(t.matmul(bcB[:, 0:128], g_onf[:1, :], cl1p[:1, :], start=True, stop=True),
                s_pe, 1)
        block.tensor(f)
        tr_rdy = s_pe.v - 1
        dec_mm = s_pe.v
        war["ftr_pe"] = tr_rdy
        war["cl1p_pe"] = s_pe.v

        def f(v, tr=tr_rdy, wp=war["f3t_pool"]):
            v.wait_ge(s_pe.h, tr)
            if wp:
                v.wait_ge(s_pool.h, wp)
            inc(v.tensor_copy(Fv(F3T_), bcA[:, 0:128]), s_dve, 1)
        block.vector(f)
        f3t_rdy = s_dve.v
        war["bcA_dve"] = s_dve.v

        def f(sc, dm=dec_mm, wd=war["dec_dve"]):
            sc.wait_ge(s_pe.h, dm)
            if wd:
                sc.wait_ge(s_dve.h, wd)
            inc(sc.activation(Fv(DEC_), bcB[:, 0:128], AF.Exp), s_act, 1)
        block.scalar(f)
        dec_rdy = s_act.v
        war["bcB_act"] = s_act.v

        for hb in range(32):
            bb = hb % 2
            g_ = hb // 4
            q = hb % 4

            def f(gp, hb=hb, g_=g_, bb=bb, pb=pb, fr=f3t_rdy, xr=x_rdy, bf=bd_free[bb]):
                gp.wait_ge(s_dve.h, fr)
                gp.wait_ge(s_hdma.h, xr)
                if bf:
                    gp.wait_ge(s_pe.h, bf)
                in0 = bass.AP(xas[pb], INTER + g_ * 128, [[9216, 128], [0, 4], [1, 128]])
                in1 = bass.AP(Ftl, F3T_ * 128 + hb * 4, [[FP, 128], [1, 4], [0, 128]])
                out = bass.AP(bds[bb], 0, [[512, 128], [128, 4], [1, 128]])
                inc(gp.tensor_tensor(out, in0, in1, ALU.mult), s_pool, 1)
            block.gpsimd(f)
            bd_rdy = s_pool.v
            war["f3t_pool"] = s_pool.v

            def f(t, hb=hb, bb=bb, pb=pb, q=q, br=bd_rdy, sf=st_free[q // 2]):
                t.wait_ge(s_pool.h, br)
                if sf:
                    t.wait_ge(s_dve.h, sf)
                for jj in range(4):
                    h = hb * 4 + jj
                    ins = t.matmul(gmps[:, q * 256 + jj * 64:q * 256 + (jj + 1) * 64],
                                   bds[bb][:, jj * 128:(jj + 1) * 128],
                                   xas[pb][:, h * 64:(h + 1) * 64], start=True, stop=True)
                    if jj == 3:
                        inc(ins, s_pe, 1)
            block.tensor(f)
            bd_free[bb] = s_pe.v
            if hb == 31:
                xa_free[pb] = s_pe.v

            if hb % 2 == 1:
                st_rdy = s_pe.v
                q0 = (hb - 1) % 4
                h0 = (hb - 1) * 4

                def f(v, sr=st_rdy, dr=dec_rdy, q0=q0, h0=h0):
                    v.wait_ge(s_pe.h, sr)
                    v.wait_ge(s_act.h, dr)
                    sv = bass.AP(S, h0 * 64, [[INTER, 128], [64, 8], [1, 64]])
                    din = bass.AP(Ftl, DEC_ * 128 + h0, [[FP, 128], [1, 8], [0, 64]])
                    v.tensor_tensor(sv, sv, din, ALU.mult)
                    ssl = S[:, h0 * 64:h0 * 64 + 512]
                    ins = v.tensor_tensor(ssl, ssl, gmps[:, q0 * 256:q0 * 256 + 512], ALU.add)
                    inc(ins, s_dve, 1)
                block.vector(f)
                st_free[q0 // 2] = s_dve.v
                war["dec_dve"] = s_dve.v

    s_fin = s_dve.v

    def f(s, sf=s_fin):
        s.wait_ge(s_dve.h, sf)
        inc(s.dma_start(out=cc_in.ap(), in_=S[:, :]), s_spill, 16)
    block.sync(f)
    ccin_rdy = s_spill.v

    def f(g, cr=ccin_rdy):
        g.wait_ge(s_spill.h, cr)
        ins = g.collective_compute(
            "AllGather", ALU.bypass,
            replica_groups=[[0, 1], [2, 3], [4, 5], [6, 7]],
            ins=[cc_in.ap().opt()], outs=[cc_out.ap().opt()])
        inc(ins, s_cc, 1)
    block.gpsimd(f)
    st2a.close()
    st2r = ExitStack()
    srecv = st2r.enter_context(nc.sbuf_tensor("p2_sr", [128, INTER], F32))
    def f(s):
        s.wait_ge(s_cc.h, 1)
        inc(s.dma_start(out=srecv[:, :],
                        in_=bass.AP(cc_out, 0, [[INTER, 128], [1, INTER]])), s_hdma, 16)
    block.sync(f)
    sr_rdy = s_hdma.v

    def f(v, sr=sr_rdy):
        v.wait_ge(s_hdma.h, sr)
        v.tensor_scalar(S[:, :], srecv[:, :], g_cst[:, 1:2], None, ALU.mult)
        inc(v.tensor_copy(Sb[:, :], S[:, :]), s_dve, 1)
    block.vector(f)
    sb_ready = s_dve.v
    barrier()
    st2r.close()
    st2b = ExitStack()
    xb0 = st2b.enter_context(nc.sbuf_tensor("b_x0", [128, 9216], BF16))
    xb1 = st2b.enter_context(nc.sbuf_tensor("b_x1", [128, 9216], BF16))
    bcm0 = st2b.enter_context(nc.sbuf_tensor("b_bc0", [128, 2048], BF16))
    bcm1 = st2b.enter_context(nc.sbuf_tensor("b_bc1", [128, 2048], BF16))
    gT = st2b.enter_context(nc.sbuf_tensor("b_gT", [128, INTER], BF16))
    zbb0 = st2b.enter_context(nc.sbuf_tensor("b_z0", [128, INTER], BF16))
    ysb = st2b.enter_context(nc.sbuf_tensor("b_y", [128, INTER], BF16))
    yosb = st2b.enter_context(nc.sbuf_tensor("b_yo", [128, 512], BF16))
    dxsb = st2b.enter_context(nc.sbuf_tensor("b_dx", [128, 1024], BF16))
    gmm = st2b.enter_context(nc.sbuf_tensor("b_gmm", [128, 1024], BF16))
    eE0 = st2b.enter_context(nc.sbuf_tensor("b_E0", [128, 512], BF16))
    eE1 = st2b.enter_context(nc.sbuf_tensor("b_E1", [128, 512], BF16))
    mM0 = st2b.enter_context(nc.sbuf_tensor("b_M0", [128, 512], BF16))
    mM1 = st2b.enter_context(nc.sbuf_tensor("b_M1", [128, 512], BF16))
    bb0 = st2b.enter_context(nc.sbuf_tensor("b_b0", [128, 512], BF16))
    bb1 = st2b.enter_context(nc.sbuf_tensor("b_b1", [128, 512], BF16))
    c1p0 = st2b.enter_context(nc.sbuf_tensor("b_c0", [1, 1024], F32))
    c1p1 = st2b.enter_context(nc.sbuf_tensor("b_c1", [1, 1024], F32))
    ssq = st2b.enter_context(nc.sbuf_tensor("b_ssq", [128, 4], F32))

    xbs = (xb0, xb1)
    bcms = (bcm0, bcm1)
    zbs = (zbb0, zbb0)
    Es = (eE0, eE1)
    Ms = (mM0, mM1)
    bds = (bb0, bb1)
    c1ps = (c1p0, c1p1)

    xb_free = [0, 0]
    bcm_free = [0, 0]
    gT_free = 0
    zb_free = [0, 0]
    E_free = [0, 0]
    M_free = [0, 0]
    bd_free = [0, 0]
    c1p_free = [0, 0]
    ydyo_free = [0, 0]
    bc_free = [0, 0]
    dx_free = [0, 0, 0, 0]
    sb_rdy_c = sb_ready

    for c in range(NCHUNK):
        pb = c % 2
        csl = slice(c * 128, (c + 1) * 128)

        def f(s, c=c, pb=pb, xf=xb_free[pb], bf=bcm_free[pb],
              gf=gT_free, clf=war["cl1p_pe"]):
            if xf:
                s.wait_ge(s_dve.h, xf)
            inc(s.dma_start(out=xbs[pb][:, :],
                            in_=bass.AP(sp_hbc, c * 128, [[TOK, 9216], [1, 128]]),
                            transpose=True), s_hdma, 16)
            if bf:
                s.wait_ge(s_pe.h, bf)
            inc(s.dma_start(out=bcms[pb][:, :],
                            in_=bass.AP(sp_hbc, INTER * TOK + c * 128,
                                        [[TOK, 128], [128 * TOK, 16], [1, 128]])),
                s_hdma, 16)
            if clf:
                s.wait_ge(s_pe.h, clf)
            inc(s.dma_start(out=cl1p[:, :],
                            in_=bass.AP(sp_cumA, c * 128 + 127, [[0, 1], [TOK, 128], [1, 1]]),
                            allow_slow_non_contiguous=True), s_hdma, 16)
            if gf:
                s.wait_ge(s_dve.h, gf)
            inc(s.dma_start(out=gT[:, :],
                            in_=bass.AP(sp_gate, c * 128, [[TOK, INTER], [1, 128]]),
                            transpose=True), s_hdma, 16)
        block.sync(f)
        gT_rdy = s_hdma.v
        cl_rdy = s_hdma.v - 16
        bcm_rdy = s_hdma.v - 32
        x_rdy = s_hdma.v - 48

        def f(sc, c=c, csl=csl, w1=war["ftr_pe"], w2=war["e3_dve"]):
            if w1:
                sc.wait_ge(s_pe.h, w1)
            if w2:
                sc.wait_ge(s_dve.h, w2)
            sc.activation(Fv(F2_), cumA[:, csl], AF.Exp)
            sc.activation(Fv(E3_), cumA[:, csl], AF.Exp, scale=-1.0,
                          bias=cumA[:, c * 128 + 127:c * 128 + 128])
            inc(sc.activation(Fv(F5_), dts[:, csl], AF.Ln), s_act, 1)
        block.scalar(f)
        fa_rdy = s_act.v

        def f(v, csl=csl, fa=fa_rdy):
            v.wait_ge(s_act.h, fa)
            v.tensor_mul(Fv(F3_), Fv(E3_), dts[:, csl])
            inc(v.tensor_sub(Fv(F5_), Fv(F5_), cumA[:, csl]), s_dve, 1)
        block.vector(f)
        ff_rdy = s_dve.v
        war["e3_dve"] = s_dve.v

        def f(t, fr=ff_rdy, clr=cl_rdy, b0=bc_free[0], b1=bc_free[1]):
            if b0:
                t.wait_ge(s_act.h, b0)
            t.wait_ge(s_dve.h, fr)
            t.transpose(bcA[:, 0:128], Fv(F3_), g_idf[:, :])
            t.transpose(bcA[:, 128:256], Fv(F2_), g_idf[:, :])
            inc(t.transpose(bcA[:, 256:384], Fv(F5_), g_idf[:, :]), s_pe, 1)
            if b1:
                t.wait_ge(s_act.h, b1)
            t.wait_ge(s_hdma.h, clr)
            inc(t.matmul(bcB[:, 0:128], g_onf[:1, :], cl1p[:1, :], start=True, stop=True),
                s_pe, 1)
        block.tensor(f)
        tr_rdy = s_pe.v - 1
        dec_mm = s_pe.v
        war["ftr_pe"] = tr_rdy
        war["cl1p_pe"] = s_pe.v

        def f(v, tr=tr_rdy, wp=war["f3t_pool"], wa=war["f5t_act"]):
            v.wait_ge(s_pe.h, tr)
            if wp:
                v.wait_ge(s_pool.h, wp)
            if wa:
                v.wait_ge(s_act.h, wa)
            v.tensor_copy(Fv(F3T_), bcA[:, 0:128])
            v.tensor_copy(Fv(F2T_), bcA[:, 128:256])
            inc(v.tensor_copy(Fv(F5T_), bcA[:, 256:384]), s_dve, 1)
        block.vector(f)
        ftr_cp = s_dve.v

        def f(sc, dm=dec_mm, wd=war["dec_dve"]):
            sc.wait_ge(s_pe.h, dm)
            if wd:
                sc.wait_ge(s_dve.h, wd)
            inc(sc.activation(Fv(DEC_), bcB[:, 0:128], AF.Exp), s_act, 1)
        block.scalar(f)
        dec_rdy = s_act.v
        war["bcB_act"] = s_act.v

        def f(t, br=bcm_rdy, pb=pb, s0=st_free[0], s1=st_free[1]):
            t.wait_ge(s_hdma.h, br)
            if s0:
                t.wait_ge(s_dve.h, s0)
            if s1:
                t.wait_ge(s_dve.h, s1)
            for g_ in range(8):
                ins = t.matmul(gmps[:, g_ * 128:(g_ + 1) * 128],
                               bcms[pb][:, g_ * 128:(g_ + 1) * 128],
                               bcms[pb][:, 1024 + g_ * 128:1024 + (g_ + 1) * 128],
                               start=True, stop=True)
                if g_ == 7:
                    inc(ins, s_pe, 1)
        block.tensor(f)
        gm_rdy = s_pe.v

        def f(v, gr=gm_rdy, wg=war["gmm_pool"]):
            v.wait_ge(s_pe.h, gr)
            if wg:
                v.wait_ge(s_pool.h, wg)
            for g_ in range(8):
                ins = v.tensor_tensor(gmm[:, g_ * 128:(g_ + 1) * 128],
                                      gmps[:, g_ * 128:(g_ + 1) * 128],
                                      g_msk[:, :], ALU.mult)
                if g_ == 7:
                    inc(ins, s_dve, 1)
        block.vector(f)
        gmm_rdy = s_dve.v

        seg_rdy = [0] * 16
        for q in (0, 1):
            def f(s, q=q, c=c, cf=c1p_free[q % 2]):
                if cf:
                    s.wait_ge(s_pe.h, cf)
                inc(s.dma_start(out=c1ps[q % 2][:, :],
                                in_=bass.AP(sp_cumA, (8 * q) * TOK + c * 128,
                                            [[0, 1], [TOK, 8], [1, 128]])),
                    s_hdma, 16)
            block.sync(f)
            seg_rdy[q] = s_hdma.v

        bc_rdy_d = {}

        def emit_bcast(hb):
            eb = hb % 2
            sg = hb // 2

            def f(t, hb=hb, eb=eb, sg=sg, sr=seg_rdy[sg], bf=bc_free[eb],
                  fc=ftr_cp, dr=dec_rdy):
                if hb == 0:
                    t.wait_ge(s_dve.h, fc)
                elif hb == 1:
                    t.wait_ge(s_act.h, dr)
                elif bf:
                    t.wait_ge(s_act.h, bf)
                t.wait_ge(s_hdma.h, sr)
                t.matmul(bc_ps[eb][:, :], g_onf[:1, :],
                         c1ps[sg % 2][:1, (hb % 2) * 512:(hb % 2 + 1) * 512],
                         start=True, stop=False)
                ins = t.matmul(bc_ps[eb][:, :], g_idb[:, :], g_mneg[:, :],
                               start=False, stop=True)
                inc(ins, s_pe, 1)
            block.tensor(f)
            bc_rdy_d[hb] = s_pe.v
            c1p_free[sg % 2] = s_pe.v

        emit_bcast(0)
        for hb in range(32):
            g_ = hb // 4
            eb = hb % 2
            j2 = (hb // 2) % 2
            q4 = hb % 4

            bc_rdy = bc_rdy_d[hb]

            def f(sc, hb=hb, eb=eb, br=bc_rdy, ef=E_free[eb], fc=ftr_cp):
                sc.wait_ge(s_pe.h, br)
                sc.wait_ge(s_dve.h, fc)
                if ef:
                    sc.wait_ge(s_pool.h, ef)
                for jj in range(4):
                    h = hb * 4 + jj
                    ins = sc.activation(Es[eb][:, jj * 128:(jj + 1) * 128],
                                        bc_ps[eb][:, jj * 128:(jj + 1) * 128], AF.Exp,
                                        bias=Ftl[:, F5T_ * 128 + h:F5T_ * 128 + h + 1])
                    if jj == 3:
                        inc(ins, s_act, 1)
            block.scalar(f)
            e_rdy = s_act.v
            bc_free[eb] = s_act.v
            war["f5t_act"] = s_act.v

            def f(gp, hb=hb, eb=eb, g_=g_, pb=pb, er=e_rdy, gr=gmm_rdy,
                  fc=ftr_cp, xr=x_rdy, mf=M_free[eb], bf=bd_free[eb],
                  df=dx_free[hb % 4]):
                gp.wait_ge(s_act.h, er)
                gp.wait_ge(s_dve.h, gr)
                if mf:
                    gp.wait_ge(s_pe.h, mf)
                in0 = bass.AP(Es[eb], 0, [[512, 128], [128, 4], [1, 128]])
                in1 = bass.AP(gmm, g_ * 128, [[1024, 128], [0, 4], [1, 128]])
                out = bass.AP(Ms[eb], 0, [[512, 128], [128, 4], [1, 128]])
                inc(gp.tensor_tensor(out, in0, in1, ALU.mult), s_pool, 1)
                gp.wait_ge(s_hdma.h, xr)
                if bf:
                    gp.wait_ge(s_pe.h, bf)
                in0 = bass.AP(xbs[pb], INTER + g_ * 128, [[9216, 128], [0, 4], [1, 128]])
                in1 = bass.AP(Ftl, F3T_ * 128 + hb * 4, [[FP, 128], [1, 4], [0, 128]])
                out = bass.AP(bds[eb], 0, [[512, 128], [128, 4], [1, 128]])
                inc(gp.tensor_tensor(out, in0, in1, ALU.mult), s_pool, 1)
                if df:
                    gp.wait_ge(s_dve.h, df)
                in0 = xbs[pb][:, hb * 256:(hb + 1) * 256]
                in1 = bass.AP(g_dbc, hb * 4, [[128, 128], [1, 4], [0, 64]])
                out = dxsb[:, (hb % 4) * 256:(hb % 4 + 1) * 256]
                inc(gp.tensor_tensor(out, in0, in1, ALU.mult), s_pool, 1)
            block.gpsimd(f)
            m_rdy = s_pool.v - 2
            bd_rdy = s_pool.v - 1
            dx_rdy = s_pool.v
            E_free[eb] = m_rdy
            war["f3t_pool"] = bd_rdy
            war["gmm_pool"] = m_rdy

            if hb + 1 < 32:
                emit_bcast(hb + 1)

            def f(t, hb=hb, eb=eb, j2=j2, g_=g_, q4=q4, pb=pb, br=bd_rdy,
                  yf=ydyo_free[j2], sf=st_free[q4 // 2], sbr=sb_rdy_c):
                t.wait_ge(s_pool.h, br)
                if hb == 0:
                    t.wait_ge(s_dve.h, sbr)
                if yf:
                    t.wait_ge(s_dve.h, yf)
                if sf:
                    t.wait_ge(s_dve.h, sf)
                for jj in range(4):
                    h = hb * 4 + jj
                    col8 = (h % 8) * 64
                    t.matmul(yd_ps[j2][:, col8:col8 + 64],
                             Ms[eb][:, jj * 128:(jj + 1) * 128],
                             xbs[pb][:, h * 64:(h + 1) * 64], start=True, stop=True)
                    t.matmul(yo_ps[j2][:, col8:col8 + 64],
                             bcms[pb][:, 1024 + g_ * 128:1024 + (g_ + 1) * 128],
                             Sb[:, h * 64:(h + 1) * 64], start=True, stop=True)
                    ins = t.matmul(gmps[:, q4 * 256 + jj * 64:q4 * 256 + (jj + 1) * 64],
                                   bds[eb][:, jj * 128:(jj + 1) * 128],
                                   xbs[pb][:, h * 64:(h + 1) * 64], start=True, stop=True)
                    if jj == 3:
                        inc(ins, s_pe, 1)
            block.tensor(f)
            y_rdy = s_pe.v
            M_free[eb] = s_pe.v
            bd_free[eb] = s_pe.v
            if hb == 31:
                bcm_free[pb] = s_pe.v

            if hb % 2 == 1:
                q0 = (hb - 1) % 4
                h0 = (hb - 1) * 4

                def f(v, j2=j2, q0=q0, h0=h0, yr=y_rdy, dxr=dx_rdy):
                    v.wait_ge(s_pe.h, yr)
                    v.wait_ge(s_pool.h, dxr)
                    yo = yosb[:, 0:512]
                    f2in = bass.AP(Ftl, F2T_ * 128 + h0, [[FP, 128], [1, 8], [0, 64]])
                    v.tensor_tensor(yo, yo_ps[j2][:, :], f2in, ALU.mult)
                    ysl = ysb[:, h0 * 64:h0 * 64 + 512]
                    v.scalar_tensor_tensor(ysl, yd_ps[j2][:, :], 1.0, yo, ALU.mult, ALU.add)
                    v.tensor_tensor(ysl, ysl, dxsb[:, q0 * 256:q0 * 256 + 512], ALU.add)
                    sv = bass.AP(S, h0 * 64, [[INTER, 128], [64, 8], [1, 64]])
                    din = bass.AP(Ftl, DEC_ * 128 + h0, [[FP, 128], [1, 8], [0, 64]])
                    v.tensor_tensor(sv, sv, din, ALU.mult)
                    ssl = S[:, h0 * 64:h0 * 64 + 512]
                    ins = v.tensor_tensor(ssl, ssl, gmps[:, q0 * 256:q0 * 256 + 512], ALU.add)
                    inc(ins, s_dve, 1)
                block.vector(f)
                ydyo_free[j2] = s_dve.v
                st_free[q0 // 2] = s_dve.v
                dx_free[q0] = s_dve.v
                dx_free[q0 + 1] = s_dve.v
                war["dec_dve"] = s_dve.v
                if hb == 31:
                    xb_free[pb] = s_dve.v

            if hb % 2 == 1 and (hb // 2) + 2 <= 15:
                def f(s, q=(hb // 2) + 2, c=c, cf=c1p_free[(hb // 2) % 2]):
                    if cf:
                        s.wait_ge(s_pe.h, cf)
                    inc(s.dma_start(out=c1ps[q % 2][:, :],
                                    in_=bass.AP(sp_cumA, (8 * q) * TOK + c * 128,
                                                [[0, 1], [TOK, 8], [1, 128]])),
                        s_hdma, 16)
                block.sync(f)
                seg_rdy[(hb // 2) + 2] = s_hdma.v

        y_done = s_dve.v

        def f(v):
            inc(v.tensor_copy(Sb[:, :], S[:, :]), s_dve, 1)
        block.vector(f)
        sb_rdy_c = s_dve.v

        def f(v, pb=pb, zf=zb_free[0]):
            if zf:
                v.wait_ge(s_spill.h, zf)
            ins = v.scalar_tensor_tensor(zbs[pb][:, :], ysb[:, :], 1.0, ysb[:, :],
                                         ALU.mult, ALU.mult, accum_out=ssq[:, 0:1])
            inc(ins, s_dve, 1)
        block.vector(f)
        ssq_rdy = s_dve.v

        def f(sc, sr=ssq_rdy):
            sc.wait_ge(s_dve.h, sr)
            sc.activation(ssq[:, 2:3], ssq[:, 0:1], AF.Ln, scale=g_cst[:, 4:5], bias=g_cst[:, 2:3])
            inc(sc.activation(ssq[:, 2:3], ssq[:, 2:3], AF.Exp, scale=g_cst[:, 3:4]), s_act, 1)
        block.scalar(f)
        rstd_rdy = s_act.v

        def f(v, pb=pb, rr=rstd_rdy, gr=gT_rdy):
            v.wait_ge(s_act.h, rr)
            v.wait_ge(s_hdma.h, gr)
            v.tensor_scalar(ysb[:, :], ysb[:, :], ssq[:, 2:3], None, ALU.mult)
            inc(v.tensor_tensor(zbs[pb][:, :], ysb[:, :], gT[:, :], ALU.mult), s_dve, 1)
        block.vector(f)
        z_rdy = s_dve.v
        gT_free = z_rdy

        def f(s, c=c, pb=pb, zr=z_rdy):
            s.wait_ge(s_dve.h, zr)
            inc(s.dma_start(out=bass.AP(sp_z, c * 128 * INTER, [[INTER, 128], [1, INTER]]),
                            in_=zbs[pb][:, :]), s_spill, 16)
        block.sync(f)
        zb_free[0] = s_spill.v

    def f(s):
        s.wait_ge(s_spill.h, s_spill.v)
    block.sync(f)
    barrier()
    st2b.close()
    st2.close()
    for B in range(4):
        st3o = ExitStack()
        x1a = st3o.enter_context(nc.sbuf_tensor(f"p3_x1a_{B}", [128, NKT * 512], BF16))
        st3 = ExitStack()
        zcm = st3.enter_context(nc.sbuf_tensor(f"p3_z_{B}", [128, 64 * 512], BF16))
        w0 = st3.enter_context(nc.sbuf_tensor(f"p3_w0_{B}", [128, 64 * 128], BF16))
        w1 = st3.enter_context(nc.sbuf_tensor(f"p3_w1_{B}", [128, 64 * 128], BF16))
        xr0 = st3.enter_context(nc.sbuf_tensor(f"p3_x0_{B}", [128, 512], F32))
        xr1 = st3.enter_context(nc.sbuf_tensor(f"p3_x1_{B}", [128, 512], F32))
        rb0 = st3.enter_context(nc.sbuf_tensor(f"p3_r0_{B}", [128, 512], F32))
        rb1 = st3.enter_context(nc.sbuf_tensor(f"p3_r1_{B}", [128, 512], F32))
        rbb0 = st3.enter_context(nc.sbuf_tensor(f"p3_rb0_{B}", [128, 512], BF16))
        rbb1 = st3.enter_context(nc.sbuf_tensor(f"p3_rb1_{B}", [128, 512], BF16))
        r2b0 = st3.enter_context(nc.sbuf_tensor(f"p3_q0_{B}", [128, 512], BF16))
        r2b1 = st3.enter_context(nc.sbuf_tensor(f"p3_q1_{B}", [128, 512], BF16))
        stt = st3.enter_context(nc.sbuf_tensor(f"p3_st_{B}", [128, 2048], F32))
        rl0 = st3.enter_context(nc.sbuf_tensor(f"p3_l0_{B}", [128, 512], F32))
        rl1 = st3.enter_context(nc.sbuf_tensor(f"p3_l1_{B}", [128, 512], F32))
        po0 = st3.enter_context(nc.psum_tensor(f"p3_po0_{B}", [128, 512], F32))
        po1 = st3.enter_context(nc.psum_tensor(f"p3_po1_{B}", [128, 512], F32))
        st1 = st3.enter_context(nc.psum_tensor(f"p3_s1_{B}", [128, 512], F32))
        st2 = st3.enter_context(nc.psum_tensor(f"p3_s2_{B}", [128, 512], F32))

        wb, xrs, rbs = (w0, w1), (xr0, xr1), (rb0, rb1)
        rbbs, r2bs, rls, pos = (rbb0, rbb1), (r2b0, r2b1), (rl0, rl1), (po0, po1)

        def f(s, B=B):
            for kt in range(64):
                inc(s.dma_start(out=zcm[:, kt * 512:(kt + 1) * 512],
                                in_=bass.AP(sp_z, (B * 512) * INTER + kt * 128,
                                            [[INTER, 512], [1, 128]]),
                                transpose=True), s_hdma, 16)
        block.sync(f)
        z_rdy = s_hdma.v

        po_free = [0, 0]
        po_done = {}
        rb_done = {}
        rbb_done = {}
        rbb_cons = {}
        rspill = {}

        for dm in range(32):
            pb = dm % 2

            def f(g, dm=dm, pb=pb):
                if dm >= 2:
                    g.wait_ge(s_pe.h, po_done[dm - 2])
                src = bass.AP(d_wout, dm * 128 * 64 * 128, [[64 * 128, 128], [1, 64 * 128]])
                inc(g.dma_start(out=wb[pb][:, :], in_=src), s_wld, 16)
            block.gpsimd(f)
            w_rdy = s_wld.v

            def f(s, dm=dm, pb=pb, B=B):
                if dm >= 2:
                    s.wait_ge(s_dve.h, rb_done[dm - 2])
                inc(s.dma_start(out=xrs[pb][:, :],
                                in_=bass.AP(d_xT, dm * 128 * (TOK + 3) + 3 + B * 512,
                                            [[TOK + 3, 128], [1, 512]])), s_hdma, 16)
            block.sync(f)
            xr_rdy = s_hdma.v

            def f(t, dm=dm, pb=pb, wr=w_rdy, zr=z_rdy, pf=po_free[pb]):
                t.wait_ge(s_wld.h, wr)
                if dm == 0:
                    t.wait_ge(s_hdma.h, zr)
                if pf:
                    t.wait_ge(s_dve.h, pf)
                for kt in range(64):
                    ins = t.matmul(pos[pb][:, :], wb[pb][:, kt * 128:(kt + 1) * 128],
                                   zcm[:, kt * 512:(kt + 1) * 512],
                                   start=(kt == 0), stop=(kt == 63))
                    if kt == 63:
                        inc(ins, s_pe, 1)
            block.tensor(f)
            po_done[dm] = s_pe.v

            if dm >= 1:
                def f(t, dm=dm, pb=pb, rc=rbb_done[dm - 1]):
                    t.wait_ge(s_act.h, rc)
                    t.matmul(st1[:, :], g_onb[:, :], rbbs[1 - pb][:, :],
                             start=(dm - 1 == 0), stop=False, skip_group_check=True)
                    ins = t.matmul(st2[:, :], g_onb[:, :], r2bs[1 - pb][:, :],
                                   start=(dm - 1 == 0), stop=False, skip_group_check=True)
                    inc(ins, s_pe, 1)
                block.tensor(f)
                rbb_cons[dm - 1] = s_pe.v

            def f(v, dm=dm, pb=pb, pr=po_done[dm], xr=xr_rdy):
                v.wait_ge(s_pe.h, pr)
                v.wait_ge(s_hdma.h, xr)
                if dm >= 2:
                    v.wait_ge(s_spill.h, rspill[dm - 2])
                inc(v.tensor_add(rbs[pb][:, :], pos[pb][:, :], xrs[pb][:, :]), s_dve, 1)
            block.vector(f)
            rb_done[dm] = s_dve.v
            po_free[pb] = s_dve.v

            def f(sc, dm=dm, pb=pb, rd=rb_done[dm]):
                sc.wait_ge(s_dve.h, rd)
                if dm >= 2:
                    sc.wait_ge(s_pe.h, rbb_cons[dm - 2])
                sc.activation(rbbs[pb][:, :], rbs[pb][:, :], AF.Copy)
                inc(sc.activation(r2bs[pb][:, :], rbs[pb][:, :], AF.Square), s_act, 1)
            block.scalar(f)
            rbb_done[dm] = s_act.v

            def f(s, dm=dm, pb=pb, rd=rb_done[dm]):
                s.wait_ge(s_dve.h, rd)
                inc(s.dma_start(out=bass.AP(sp_r, dm * 128 * 512, [[512, 128], [1, 512]]),
                                in_=rbs[pb][:, :]), s_spill, 16)
            block.sync(f)
            rspill[dm] = s_spill.v

        def f(t, rc=rbb_done[31]):
            t.wait_ge(s_act.h, rc)
            t.matmul(st1[:, :], g_onb[:, :], rbbs[1][:, :],
                     start=False, stop=True, skip_group_check=True)
            inc(t.matmul(st2[:, :], g_onb[:, :], r2bs[1][:, :],
                         start=False, stop=True, skip_group_check=True), s_pe, 1)
        block.tensor(f)
        st_done = s_pe.v
        rbb_cons[31] = s_pe.v

        def f(v, sd=st_done):
            v.wait_ge(s_pe.h, sd)
            v.tensor_scalar(stt[:, 0:512], st1[:, :], 1.0 / DIM, None, ALU.mult)
            inc(v.tensor_scalar(stt[:, 512:1024], st2[:, :], 1.0 / DIM, None, ALU.mult), s_dve, 1)
            v.wait_ge(s_dve.h, s_dve.v)
            inc(v.tensor_mul(stt[:, 1024:1536], stt[:, 0:512], stt[:, 0:512]), s_dve, 1)
            v.wait_ge(s_dve.h, s_dve.v)
            inc(v.tensor_sub(stt[:, 1024:1536], stt[:, 512:1024], stt[:, 1024:1536]),
                s_dve, 1)
        block.vector(f)
        var_rdy = s_dve.v

        def f(sc, vr=var_rdy):
            sc.wait_ge(s_dve.h, vr)
            inc(sc.activation(stt[:, 1536:2048], stt[:, 1024:1536], AF.Ln, bias=g_cst[:, 2:3]), s_act, 1)
            sc.wait_ge(s_act.h, s_act.v)
            inc(sc.activation(stt[:, 1536:2048], stt[:, 1536:2048], AF.Exp, scale=g_cst[:, 3:4]),
                s_act, 1)
        block.scalar(f)
        rstd_rdy = s_act.v

        x1_done = {}
        x1spill = {}
        for dm in range(32):
            pb = dm % 2

            def f(s, dm=dm, pb=pb, sp=rspill[dm]):
                s.wait_ge(s_spill.h, sp)
                if dm >= 2:
                    s.wait_ge(s_dve.h, x1_done[dm - 2])
                    s.wait_ge(s_spill.h, x1spill[dm - 2])
                inc(s.dma_start(out=rls[pb][:, :],
                                in_=bass.AP(sp_r, dm * 128 * 512, [[512, 128], [1, 512]])),
                    s_hdma, 16)
            block.sync(f)
            rl_rdy = s_hdma.v

            def f(v, dm=dm, pb=pb, rr=rl_rdy, sr=rstd_rdy):
                v.wait_ge(s_hdma.h, rr)
                v.wait_ge(s_act.h, sr)
                v.tensor_sub(rls[pb][:, :], rls[pb][:, :], stt[:, 0:512])
                v.tensor_mul(rls[pb][:, :], rls[pb][:, :], stt[:, 1536:2048])
                inc(v.tensor_copy(x1a[:, dm * 512:(dm + 1) * 512], rls[pb][:, :]), s_dve, 1)
            block.vector(f)
            x1_done[dm] = s_dve.v

            def f(s, dm=dm, pb=pb, xd=x1_done[dm], B=B):
                s.wait_ge(s_dve.h, xd)
                inc(s.dma_start(out=bass.AP(sp_x1, dm * 128 * TOK + B * 512,
                                            [[TOK, 128], [1, 512]]),
                                in_=rls[pb][:, :]), s_spill, 16)
            block.sync(f)
            x1spill[dm] = s_spill.v

        def f(s):
            s.wait_ge(s_spill.h, s_spill.v)
        block.sync(f)
        barrier()
        st3.close()
        st4 = ExitStack()
        w0 = st4.enter_context(nc.sbuf_tensor(f"p4_w0_{B}", [128, NKT * 128], BF16))
        w1 = st4.enter_context(nc.sbuf_tensor(f"p4_w1_{B}", [128, NKT * 128], BF16))
        lmb = st4.enter_context(nc.sbuf_tensor(f"p4_lm_{B}", [16, 512], BF16))
        xr0 = st4.enter_context(nc.sbuf_tensor(f"p4_x0_{B}", [128, 512], F32))
        xr1 = st4.enter_context(nc.sbuf_tensor(f"p4_x1_{B}", [128, 512], F32))
        rb0 = st4.enter_context(nc.sbuf_tensor(f"p4_r0_{B}", [128, 512], F32))
        rb1 = st4.enter_context(nc.sbuf_tensor(f"p4_r1_{B}", [128, 512], F32))
        rbb0 = st4.enter_context(nc.sbuf_tensor(f"p4_rb0_{B}", [128, 512], BF16))
        rbb1 = st4.enter_context(nc.sbuf_tensor(f"p4_rb1_{B}", [128, 512], BF16))
        r2b0 = st4.enter_context(nc.sbuf_tensor(f"p4_q0_{B}", [128, 512], BF16))
        r2b1 = st4.enter_context(nc.sbuf_tensor(f"p4_q1_{B}", [128, 512], BF16))
        stt = st4.enter_context(nc.sbuf_tensor(f"p4_st_{B}", [128, 2048], F32))
        rl0 = st4.enter_context(nc.sbuf_tensor(f"p4_l0_{B}", [128, 512], F32))
        rl1 = st4.enter_context(nc.sbuf_tensor(f"p4_l1_{B}", [128, 512], F32))
        ev0 = st4.enter_context(nc.sbuf_tensor(f"p4_e0_{B}", [128, 512], F32))
        ev1 = st4.enter_context(nc.sbuf_tensor(f"p4_e1_{B}", [128, 512], F32))
        po0 = st4.enter_context(nc.psum_tensor(f"p4_po0_{B}", [128, 512], F32))
        po1 = st4.enter_context(nc.psum_tensor(f"p4_po1_{B}", [128, 512], F32))
        st1 = st4.enter_context(nc.psum_tensor(f"p4_s1_{B}", [128, 512], F32))
        st2 = st4.enter_context(nc.psum_tensor(f"p4_s2_{B}", [128, 512], F32))
        lmp = st4.enter_context(nc.psum_tensor(f"p4_lp_{B}", [16, 512], F32))
        tp0 = st4.enter_context(nc.psum_tensor(f"p4_t0_{B}", [128, 128], F32))
        tp1 = st4.enter_context(nc.psum_tensor(f"p4_t1_{B}", [128, 128], F32))

        wb, xrs, rbs = (w0, w1), (xr0, xr1), (rb0, rb1)
        rbbs, r2bs, rls = (rbb0, rbb1), (r2b0, r2b1), (rl0, rl1)
        pos, evs, tps = (po0, po1), (ev0, ev1), (tp0, tp1)

        def f(t):
            for kt in range(NKT):
                ins = t.matmul(lmp[:, :], g_aeff[:, kt * 16:(kt + 1) * 16],
                               x1a[:, kt * 512:(kt + 1) * 512],
                               start=(kt == 0), stop=(kt == NKT - 1))
                if kt == NKT - 1:
                    inc(ins, s_pe, 1)
        block.tensor(f)
        lm_rdy = s_pe.v

        def f(v, lr=lm_rdy):
            v.wait_ge(s_pe.h, lr)
            inc(v.tensor_copy(lmb[:, :], lmp[:, :]), s_dve, 1)
        block.vector(f)
        lmb_rdy = s_dve.v

        po_free = [0, 0]
        po_done = {}
        rb_done = {}
        rbb_done = {}
        rbb_cons = {}
        rspill = {}
        for dm in range(32):
            pb = dm % 2

            def f(g, dm=dm, pb=pb):
                if dm >= 2:
                    g.wait_ge(s_pe.h, po_done[dm - 2])
                src = bass.AP(d_wffn, dm * 128 * NKT * 128, [[NKT * 128, 128], [1, NKT * 128]])
                inc(g.dma_start(out=wb[pb][:, :], in_=src), s_wld, 16)
            block.gpsimd(f)
            w_rdy = s_wld.v

            def f(s, dm=dm, pb=pb, B=B):
                if dm >= 2:
                    s.wait_ge(s_dve.h, rb_done[dm - 2])
                inc(s.dma_start(out=xrs[pb][:, :],
                                in_=bass.AP(sp_x1, dm * 128 * TOK + B * 512,
                                            [[TOK, 128], [1, 512]])), s_hdma, 16)
            block.sync(f)
            xr_rdy = s_hdma.v

            def f(t, dm=dm, pb=pb, wr=w_rdy, lr=lmb_rdy, pf=po_free[pb]):
                t.wait_ge(s_wld.h, wr)
                t.wait_ge(s_dve.h, lr)
                if pf:
                    t.wait_ge(s_dve.h, pf)
                nl = "no_lora" in flags
                for kt in range(NKT):
                    ins = t.matmul(pos[pb][:, :], wb[pb][:, kt * 128:(kt + 1) * 128],
                                   x1a[:, kt * 512:(kt + 1) * 512],
                                   start=(kt == 0), stop=(nl and kt == NKT - 1))
                    if nl and kt == NKT - 1:
                        inc(ins, s_pe, 1)
                if not nl:
                    ins = t.matmul(pos[pb][:, :], g_beff[:, dm * 128:(dm + 1) * 128],
                                   lmb[:, :], start=False, stop=True)
                    inc(ins, s_pe, 1)
            block.tensor(f)
            po_done[dm] = s_pe.v

            if dm >= 1:
                def f(t, dm=dm, pb=pb, rc=rbb_done[dm - 1]):
                    t.wait_ge(s_act.h, rc)
                    t.matmul(st1[:, :], g_onb[:, :], rbbs[1 - pb][:, :],
                             start=(dm - 1 == 0), stop=False, skip_group_check=True)
                    ins = t.matmul(st2[:, :], g_onb[:, :], r2bs[1 - pb][:, :],
                                   start=(dm - 1 == 0), stop=False, skip_group_check=True)
                    inc(ins, s_pe, 1)
                block.tensor(f)
                rbb_cons[dm - 1] = s_pe.v

            def f(v, dm=dm, pb=pb, pr=po_done[dm], xr=xr_rdy):
                v.wait_ge(s_pe.h, pr)
                v.wait_ge(s_hdma.h, xr)
                if dm >= 2:
                    v.wait_ge(s_spill.h, rspill[dm - 2])
                ins = v.scalar_tensor_tensor(rbs[pb][:, :], pos[pb][:, :],
                                             g_fb[:, dm:dm + 1], xrs[pb][:, :],
                                             ALU.add, ALU.add)
                inc(ins, s_dve, 1)
            block.vector(f)
            rb_done[dm] = s_dve.v
            po_free[pb] = s_dve.v

            def f(sc, dm=dm, pb=pb, rd=rb_done[dm]):
                sc.wait_ge(s_dve.h, rd)
                if dm >= 2:
                    sc.wait_ge(s_pe.h, rbb_cons[dm - 2])
                sc.activation(rbbs[pb][:, :], rbs[pb][:, :], AF.Copy)
                inc(sc.activation(r2bs[pb][:, :], rbs[pb][:, :], AF.Square), s_act, 1)
            block.scalar(f)
            rbb_done[dm] = s_act.v

            def f(s, dm=dm, pb=pb, rd=rb_done[dm]):
                s.wait_ge(s_dve.h, rd)
                inc(s.dma_start(out=bass.AP(sp_r, dm * 128 * 512, [[512, 128], [1, 512]]),
                                in_=rbs[pb][:, :]), s_spill, 16)
            block.sync(f)
            rspill[dm] = s_spill.v

        def f(t, rc=rbb_done[31]):
            t.wait_ge(s_act.h, rc)
            t.matmul(st1[:, :], g_onb[:, :], rbbs[1][:, :],
                     start=False, stop=True, skip_group_check=True)
            inc(t.matmul(st2[:, :], g_onb[:, :], r2bs[1][:, :],
                         start=False, stop=True, skip_group_check=True), s_pe, 1)
        block.tensor(f)
        st_done = s_pe.v
        rbb_cons[31] = s_pe.v

        def f(v, sd=st_done):
            v.wait_ge(s_pe.h, sd)
            v.tensor_scalar(stt[:, 0:512], st1[:, :], 1.0 / DIM, None, ALU.mult)
            inc(v.tensor_scalar(stt[:, 512:1024], st2[:, :], 1.0 / DIM, None, ALU.mult), s_dve, 1)
            v.wait_ge(s_dve.h, s_dve.v)
            inc(v.tensor_mul(stt[:, 1024:1536], stt[:, 0:512], stt[:, 0:512]), s_dve, 1)
            v.wait_ge(s_dve.h, s_dve.v)
            inc(v.tensor_sub(stt[:, 1024:1536], stt[:, 512:1024], stt[:, 1024:1536]),
                s_dve, 1)
        block.vector(f)
        var_rdy = s_dve.v

        def f(sc, vr=var_rdy):
            sc.wait_ge(s_dve.h, vr)
            inc(sc.activation(stt[:, 1536:2048], stt[:, 1024:1536], AF.Ln, bias=g_cst[:, 2:3]), s_act, 1)
            sc.wait_ge(s_act.h, s_act.v)
            inc(sc.activation(stt[:, 1536:2048], stt[:, 1536:2048], AF.Exp, scale=g_cst[:, 3:4]),
                s_act, 1)
        block.scalar(f)
        rstd_rdy = s_act.v

        x2_done = {}
        tp_free = [0, 0]
        ev_free = [0, 0, 0, 0]
        for dm in range(32):
            pb = dm % 2

            def f(s, dm=dm, pb=pb, sp=rspill[dm]):
                s.wait_ge(s_spill.h, sp)
                if dm >= 2:
                    s.wait_ge(s_pe.h, x2_done[dm - 2])
                inc(s.dma_start(out=rls[pb][:, :],
                                in_=bass.AP(sp_r, dm * 128 * 512, [[512, 128], [1, 512]])),
                    s_hdma, 16)
            block.sync(f)
            rl_rdy = s_hdma.v

            def f(v, pb=pb, rr=rl_rdy, sr=rstd_rdy):
                v.wait_ge(s_hdma.h, rr)
                v.wait_ge(s_act.h, sr)
                v.tensor_sub(rls[pb][:, :], rls[pb][:, :], stt[:, 0:512])
                inc(v.tensor_mul(rls[pb][:, :], rls[pb][:, :], stt[:, 1536:2048]), s_dve, 1)
            block.vector(f)
            x2_rdy = s_dve.v

            for j in range(4):
                tb = j % 2

                def f(t, pb=pb, j=j, tb=tb, xr=x2_rdy, tf=tp_free[tb]):
                    t.wait_ge(s_dve.h, xr)
                    if tf:
                        t.wait_ge(s_dve.h, tf)
                    inc(t.transpose(tps[tb][:, :], rls[pb][:, j * 128:(j + 1) * 128],
                                    g_idf[:, :]), s_pe, 1)
                block.tensor(f)
                tp_rdy = s_pe.v
                if j == 3:
                    x2_done[dm] = s_pe.v

                def f(v, tb=tb, j=j, tr=tp_rdy, ef=ev_free[j]):
                    v.wait_ge(s_pe.h, tr)
                    if ef:
                        v.wait_ge(s_out.h, ef)
                    inc(v.tensor_copy(evs[tb][:, j * 128:(j + 1) * 128], tps[tb][:, :]),
                        s_dve, 1)
                block.vector(f)
                evr = s_dve.v
                tp_free[tb] = s_dve.v

                def f(s, dm=dm, j=j, tb=tb, er=evr, B=B):
                    s.wait_ge(s_dve.h, er)
                    dst = bass.AP(d_out, (B * 512 + j * 128) * DIM + dm * 128,
                                  [[DIM, 128], [1, 128]])
                    inc(s.dma_start(out=dst, in_=evs[tb][:, j * 128:(j + 1) * 128]),
                        s_out, 16)
                block.sync(f)
                ev_free[j] = s_out.v
        barrier()
        st4.close()
        st3o.close()
    def f(s):
        s.wait_ge(s_out.h, s_out.v)
    block.sync(f)

    blk_cm.__exit__(None, None, None)
    es.close()
    return nc


def kernel(L0_attn_hidden_states, L0_attn_in_proj_w, L0_attn_conv_w, L0_attn_conv_b,
           L0_attn_dt_bias, L0_attn_A_log, L0_attn_D, L0_attn_norm_w,
           L0_attn_out_proj_w, L0_ffn_weight, L0_ffn_bias, L0_ffn_lora_A,
           L0_ffn_lora_B, L0_ffn_lora_start_indices):
    X = np.ascontiguousarray(np.asarray(L0_attn_hidden_states, np.float32))
    Win = np.asarray(L0_attn_in_proj_w, np.float32)
    Wout = np.asarray(L0_attn_out_proj_w, np.float32)
    Wffn = np.asarray(L0_ffn_weight, np.float32)

    order = np.concatenate([np.arange(INTER, INTER + CONV_DIM),
                            np.arange(INTER + CONV_DIM, NPROJ),
                            np.arange(0, INTER)])
    Wsel = Win.T[:, order]
    win_t = np.ascontiguousarray(Wsel.reshape(NKT, 128, NTILE, 128).transpose(2, 1, 0, 3))
    wout_t = np.ascontiguousarray(Wout.T.reshape(64, 128, 32, 128).transpose(2, 1, 0, 3))
    wffn_t = np.ascontiguousarray(Wffn.T.reshape(NKT, 128, 32, 128).transpose(2, 1, 0, 3))

    cw_t = np.ascontiguousarray(
        np.asarray(L0_attn_conv_w, np.float32).reshape(N_HBC_T, 128, 4)
        .transpose(1, 0, 2).reshape(128, N_HBC_T * 4))
    cb_t = np.ascontiguousarray(np.asarray(L0_attn_conv_b, np.float32).reshape(N_HBC_T, 128).T)
    nw_t = np.ascontiguousarray(np.asarray(L0_attn_norm_w, np.float32).reshape(64, 128).T)
    fb_t = np.ascontiguousarray(np.asarray(L0_ffn_bias, np.float32).reshape(32, 128).T)
    dtb = np.ascontiguousarray(np.asarray(L0_attn_dt_bias, np.float32).reshape(128, 1))
    alog = np.ascontiguousarray(np.asarray(L0_attn_A_log, np.float32).reshape(128, 1))
    D = np.asarray(L0_attn_D, np.float32)
    dbc = np.ascontiguousarray(np.tile(D[None, :], (128, 1)))
    mskt = np.ascontiguousarray(np.triu(np.ones((128, 128), np.float32)))
    idf = np.eye(128, dtype=np.float32)
    onb = np.ones((128, 128), BFNP)
    onf = np.ones((1, 128), np.float32)
    idb = np.eye(128, dtype=BFNP)
    mneg = np.ascontiguousarray(np.tile(np.tril(np.full((128, 128), -1e9, np.float32), -1),
                                        (1, 4)).astype(BFNP))
    laT = np.ascontiguousarray(np.asarray(L0_ffn_lora_A, np.float32))
    lbT = np.ascontiguousarray(np.asarray(L0_ffn_lora_B, np.float32))
    laT = np.ascontiguousarray(np.asarray(L0_ffn_lora_A, np.float32).transpose(0, 2, 1))
    lbT = np.ascontiguousarray(np.asarray(L0_ffn_lora_B, np.float32).transpose(0, 2, 1))
    lsi = np.ascontiguousarray(np.asarray(L0_ffn_lora_start_indices, np.int32).reshape(1, 4))

    nc = build_program()
    in_maps = _in_maps_from(X, win_t, wout_t, wffn_t, cw_t, cb_t, nw_t, fb_t, dtb,
                            alog, dbc, mskt, idf, onb, onf, laT, lbT, lsi, idb, mneg)
    res = run_bass_kernel_spmd(nc, in_maps, core_ids=list(range(8)))
    out = np.empty((4, 4096, DIM), np.float32)
    for c in range(8):
        b, half = c // 2, c % 2
        out[b, half * TOK:(half + 1) * TOK, :] = res.results[c]["out"]
    return out


def host_prep(inputs):
    import inspect
    sig = inspect.signature(kernel)
    kw = {k: inputs[k] for k in sig.parameters}
    L = kw
    X = np.ascontiguousarray(np.asarray(L["L0_attn_hidden_states"], np.float32))
    Win = np.asarray(L["L0_attn_in_proj_w"], np.float32)
    Wout = np.asarray(L["L0_attn_out_proj_w"], np.float32)
    Wffn = np.asarray(L["L0_ffn_weight"], np.float32)
    order = np.concatenate([np.arange(INTER, INTER + CONV_DIM),
                            np.arange(INTER + CONV_DIM, NPROJ),
                            np.arange(0, INTER)])
    Wsel = Win.T[:, order]
    win_t = np.ascontiguousarray(Wsel.reshape(NKT, 128, NTILE, 128).transpose(2, 1, 0, 3))
    wout_t = np.ascontiguousarray(Wout.T.reshape(64, 128, 32, 128).transpose(2, 1, 0, 3))
    wffn_t = np.ascontiguousarray(Wffn.T.reshape(NKT, 128, 32, 128).transpose(2, 1, 0, 3))
    cw_t = np.ascontiguousarray(
        np.asarray(L["L0_attn_conv_w"], np.float32).reshape(N_HBC_T, 128, 4)
        .transpose(1, 0, 2).reshape(128, N_HBC_T * 4))
    cb_t = np.ascontiguousarray(np.asarray(L["L0_attn_conv_b"], np.float32).reshape(N_HBC_T, 128).T)
    nw_t = np.ascontiguousarray(np.asarray(L["L0_attn_norm_w"], np.float32).reshape(64, 128).T)
    fb_t = np.ascontiguousarray(np.asarray(L["L0_ffn_bias"], np.float32).reshape(32, 128).T)
    dtb = np.ascontiguousarray(np.asarray(L["L0_attn_dt_bias"], np.float32).reshape(128, 1))
    alog = np.ascontiguousarray(np.asarray(L["L0_attn_A_log"], np.float32).reshape(128, 1))
    D = np.asarray(L["L0_attn_D"], np.float32)
    dbc = np.ascontiguousarray(np.tile(D[None, :], (128, 1)))
    mskt = np.ascontiguousarray(np.triu(np.ones((128, 128), np.float32)))
    idf = np.eye(128, dtype=np.float32)
    onb = np.ones((128, 128), BFNP)
    onf = np.ones((1, 128), np.float32)
    idb = np.eye(128, dtype=BFNP)
    mneg = np.ascontiguousarray(np.tile(np.tril(np.full((128, 128), -1e9, np.float32), -1),
                                        (1, 4)).astype(BFNP))
    laT = np.ascontiguousarray(np.asarray(L["L0_ffn_lora_A"], np.float32).transpose(0, 2, 1))
    lbT = np.ascontiguousarray(np.asarray(L["L0_ffn_lora_B"], np.float32).transpose(0, 2, 1))
    lsi = np.ascontiguousarray(np.asarray(L["L0_ffn_lora_start_indices"], np.int32).reshape(1, 4))
    return _in_maps_from(X, win_t, wout_t, wffn_t, cw_t, cb_t, nw_t, fb_t, dtb,
                         alog, dbc, mskt, idf, onb, onf, laT, lbT, lsi, idb, mneg)


def _in_maps_from(X, win_t, wout_t, wffn_t, cw_t, cb_t, nw_t, fb_t, dtb,
                  alog, dbc, mskt, idf, onb, onf, laT, lbT, lsi, idb, mneg):
    in_maps = []
    for c in range(8):
        b, half = c // 2, c % 2
        toks = X[b, half * TOK:(half + 1) * TOK, :]
        halo = np.zeros((3, DIM), np.float32) if half == 0 else X[b, TOK - 3:TOK, :]
        xt = np.ascontiguousarray(np.concatenate([halo, toks], 0).T)
        cst = np.zeros((128, 8), np.float32)
        cst[:, 0] = float(b)
        cst[:, 1] = float(half)
        cst[:, 2] = EPS
        cst[:, 3] = -0.5
        cst[:, 4] = 1.0 / INTER
        in_maps.append({
            "xT": xt, "w_in": win_t, "w_out": wout_t, "w_ffn": wffn_t,
            "cw": cw_t, "cb": cb_t, "nw": nw_t, "fb": fb_t, "dtb": dtb,
            "alog": alog, "dbc": dbc, "mskt": mskt, "cst": cst,
            "idf": idf, "onb": onb, "onf": onf, "idb": idb, "mneg": mneg,
            "laT": laT, "lbT": lbT, "lsi": lsi,
        })
    return in_maps
